# version 18
# baseline (speedup 1.0000x reference)
import math
from collections import defaultdict
from contextlib import ExitStack

import numpy as np
import ml_dtypes

import concourse.bass as bass
import concourse.mybir as mybir
from concourse.bass_utils import run_bass_kernel_spmd

F32 = mybir.dt.float32
BF16 = mybir.dt.bfloat16
AF = mybir.ActivationFunctionType
ALU = mybir.AluOpType
AX = mybir.AxisListType

D = 4096
T = 8192
H = 32
HD = 128
NT = 64
NJ = 8
NCORE = 8
NE = 64
KTOP = 8
SCL = 1.0 / math.sqrt(128.0)
EPS = 1e-5
NEGC = -12.0


def _batches():
    sizes = [(8 - c // 8) * 128 for c in range(64)]
    batches, cur, cols = [], [], 0
    for c, s in enumerate(sizes):
        if cols + s > 1024 or (cur and s != cur[0][2]):
            batches.append(cur)
            cur, cols = [], 0
        cur.append((c, cols, s))
        cols += s
    batches.append(cur)
    return batches


def build_nc(debug=False):
    nc = bass.Bass("TRN2", target_bir_lowering=False, debug=False)
    dr = lambda n, s, dt=F32, kind="ExternalInput": nc.dram_tensor(n, s, dt, kind=kind)

    xfm_own_h = dr("xfm_own", [NJ, 128, D])
    xtm_own_h = dr("xtm_own", [NJ, 128, D])
    bigwq_h = dr("bigwq", [32, 128, D])
    bigwo_h = dr("bigwo", [16, 128, 8192])
    wkt_h = dr("wkt", [128, D])
    wvt_h = dr("wvt", [128, D])
    rwt_h = dr("rwt", [128, 2048])
    eb65_h = dr("eb65", [65, D])
    wqb_h = dr("wqb", [128, 32])
    wkb_h = dr("wkb", [128, 1])
    wvb_h = dr("wvb", [1, 128])
    rbb_h = dr("rbb", [128, NE])
    identf_h = dr("identf", [128, 128])
    mask8_h = dr("mask8", [128, 1024], BF16)
    out_h = dr("out", [NJ, 128, D], F32, kind="ExternalOutput")
    kown_d = nc.dram_tensor("kown_b", [128, 1024], BF16)
    kgath_d = nc.dram_tensor("kgath_b", [1024, 1024], BF16, addr_space="Shared")
    vown_d = nc.dram_tensor("vown_b", [1024, 128], BF16)
    vgath_d = nc.dram_tensor("vgath_b", [T, 128], BF16, addr_space="Shared")
    dbg_h = {}
    if debug:
        for n, s, dt in [("dbg_rstd", [128, NJ], F32), ("dbg_nmr", [128, NJ], F32),
                         ("dbg_xn", [128, 32768], BF16), ("dbg_K", [128, T], BF16),
                         ("dbg_V", [128, T], BF16), ("dbg_q", [128, 32768], BF16),
                         ("dbg_o", [128, 32768], BF16), ("dbg_wdt", [65, 1024], BF16),
                         ("dbg_wsum", [128, NJ], F32), ("dbg_s1", [128, NJ], F32),
                         ("dbg_s2", [128, NJ], F32), ("dbg_sx", [128, 2], F32),
                         ("dbg_sxx", [128, 2], F32), ("dbg_mt", [128, 2], F32),
                         ("dbg_vt", [128, 2], F32), ("dbg_lt", [128, 2], F32),
                         ("dbg_bsb", [128, 512], F32), ("dbg_rows", [1, 512], F32),
                         ("dbg_lg", [128, 512], F32), ("dbg_snap", [128, 512], F32),
                         ("dbg_nm8", [128, NJ], F32), ("dbg_ss8", [128, NJ], F32),
                         ("dbg_rs8", [128, NJ], F32), ("dbg_e1", [128, 2048], F32),
                         ("dbg_e2", [128, 2048], F32), ("dbg_e3", [128, 256], F32),
                         ("dbg_s1b", [128, NJ], F32)]:
            dbg_h[n] = dr(n, s, dt, kind="ExternalOutput")

    with ExitStack() as stack:
        sb = lambda n, s, dt: stack.enter_context(nc.sbuf_tensor(n, s, dt))
        psa = lambda n: stack.enter_context(nc.psum_tensor(n, [128, 1024], F32))
        mksem = lambda n: stack.enter_context(nc.semaphore(n))

        identf = sb("identf_s", [128, 128], F32)
        mask8 = sb("mask8_s", [128, 1024], BF16)
        wqb = sb("wqb_s", [128, 32], F32)
        wkb = sb("wkb_s", [128, 1], F32)
        wvb = sb("wvb_s", [1, 128], BF16)
        rbb = sb("rbb_s", [128, NE], F32)
        wdT65 = sb("wdT65_s", [65, 1024], BF16)
        neg12 = sb("neg12_s", [128, 1], F32)
        onescb = sb("onescb_s", [128, 1], BF16)
        onesrb = sb("onesrb_s", [1, 128], BF16)
        onesrf = sb("onesrf_s", [1, 128], F32)
        rstd_all = sb("rstd_all_s", [128, NJ], F32)
        nmr_all = sb("nmr_all_s", [128, NJ], F32)
        wsum = sb("wsum_s", [128, NJ], F32)
        s1a = sb("s1a_s", [128, NJ], F32)
        s2a = sb("s2a_s", [128, NJ], F32)
        sx2 = sb("sx2_s", [128, 2], F32)
        sxx2 = sb("sxx2_s", [128, 2], F32)
        mt2 = sb("mt2_s", [128, 2], F32)
        mq2 = sb("mq2_s", [128, 2], F32)
        vt2 = sb("vt2_s", [128, 2], F32)
        lt2 = sb("lt2_s", [128, 2], F32)
        rst2 = sb("rst2_s", [128, 2], F32)
        nmr2 = sb("nmr2_s", [128, 2], F32)
        logits_sb = sb("logits_s", [128, NE], F32)
        if debug:
            dbg_lg_sb = sb("dbg_lg_sb", [128, 512], F32)
            dbg_snap_sb = sb("dbg_snap_sb", [128, 512], F32)
            dbg_nm8_sb = sb("dbg_nm8_sb", [128, NJ], F32)
            dbg_ss8_sb = sb("dbg_ss8_sb", [128, NJ], F32)
            dbg_rs8_sb = sb("dbg_rs8_sb", [128, NJ], F32)
        s_sb = sb("ssm_s", [128, NE], F32)
        swork = sb("swork_s", [128, NE], F32)
        keep = sb("keep_s", [128, NE], F32)
        mske = sb("mske_s", [128, NE], F32)
        wdense = sb("wdense_s", [128, NE], F32)
        m8 = sb("m8_s", [128, 1], F32)
        ssum8 = sb("ssum8_s", [128, NJ], F32)
        rsr8 = sb("rsr8_s", [128, NJ], F32)
        m8w = sb("m8w_s", [128, NJ], F32)
        nmax = sb("nmax_s", [128, 1], F32)
        ssum = sb("ssum_s", [128, 1], F32)
        rsr = sb("rsr_s", [128, 1], F32)

        K_fm = sb("K_fm", [128, T], BF16)
        V_tm = sb("V_tm", [128, T], BF16)
        qpool = sb("qpool", [128, 32768], BF16)
        xn_own = sb("xn_own", [128, 32768], BF16)
        stream = sb("stream", [128, 16384], BF16)

        qpool_f = qpool.bitcast(F32)
        xn_f = xn_own.bitcast(F32)
        stream_f = stream.bitcast(F32)
        kv_f = K_fm.bitcast(F32)

        assert list(qpool_f.shape) == [128, 16384], qpool_f.shape

        xfm_buf = [qpool[:, k * 4096:(k + 1) * 4096] for k in range(3)]
        xsq_buf = [qpool[:, 12288 + k * 4096:12288 + (k + 1) * 4096] for k in range(3)]
        rwt_bf = stream[:, 0:2048]
        wq_slab = [stream[:, 0:4096], stream[:, 4096:8192]]
        ebuf = [stream[:, 8192:9216], stream[:, 9216:10240], stream[:, 0:1024],
                stream[:, 1024:2048], stream[:, 2048:3072], stream[:, 3072:4096],
                stream[:, 4096:5120]]
        kown_sb = stream[:, 8192:9216]
        vown_sb = stream[:, 9216:10240]
        bsb = stream_f[:, 5120:6144]
        rec = stream_f[0:1, 6144:7168]
        lnrow = stream_f[0:1, 7168:8192]
        wo_slab = [stream[:, 0:8192], stream[:, 8192:16384]]
        stg = [xn_f[:, j * 2048:(j + 1) * 2048] for j in range(NJ)]
        xbufE = [kv_f[:, 0:2048], kv_f[:, 2048:4096]]
        eb65 = V_tm[0:65, 0:4096]
        b_sb = stream_f[:, 5120:5632]
        rows_sb = stream_f[0:1, 5632:6144]
        logits_all = stream_f[:, 5632:6144]
        s_all = stream_f[:, 4096:4608]
        swork_w = stream_f[:, 4608:5120]
        keep_w = stream_f[:, 5120:5632]

        PS = [psa(f"PS{i}") for i in range(4)]

        sems = {n: mksem(n) for n in ["ds0", "ds1", "dg0", "dg1", "dma_c", "st",
                                      "pe", "act", "vec", "pool", "coll", "kvld"]}
        cnt = defaultdict(int)
        marks = {}
        q = {"sync": [], "gpsimd": [], "tensor": [], "scalar": [], "vector": []}

        def EI(eng, bf, sem=None, k=1, mark=None):
            if eng == "vector" and sem is None:
                sem = "vec"
            if sem is not None:
                cnt[sem] += k
                if mark is not None:
                    marks[mark] = (sem, cnt[sem])
                sh = sems[sem]

                def g(e, bf=bf, sh=sh, k=k):
                    bf(e).then_inc(sh, k)
            else:
                def g(e, bf=bf):
                    bf(e)
            q[eng].append(g)
            if eng == "vector":
                W("vector", "vec", cnt["vec"])

        def W(eng, sem, v):
            if v > 0:
                q[eng].append(lambda e, sh=sems[sem], v=v: e.wait_ge(sh, v))

        def WM(eng, key):
            if key in marks:
                s, v = marks[key]
                W(eng, s, v)

        def EIG(bf, mark=None):
            EI("vector", bf, "vec", 1, mark=mark)

        def barrier():
            snap = [(s, cnt[s]) for s in sems if s not in ("coll", "kvld")]
            for eng in q:
                for s, v in snap:
                    W(eng, s, v)

        def DUMP(name, ap):
            if debug:
                EI("gpsimd", lambda g, ap=ap, h=dbg_h[name]: g.dma_start(h[:], ap), "st", 16)

        EI("sync", lambda e: e.dma_start(identf[:], identf_h[:]), "ds0", 16)
        EI("sync", lambda e: e.dma_start(mask8[:], mask8_h[:]), "ds0", 16)
        EI("sync", lambda e: e.dma_start(wqb[:], wqb_h[:]), "ds0", 16)
        EI("sync", lambda e: e.dma_start(wkb[:], wkb_h[:]), "ds0", 16)
        EI("gpsimd", lambda g: g.dma_start(wvb[:], wvb_h[:]), "dma_c", 16)
        EI("sync", lambda e: e.dma_start(rbb[:], rbb_h[:]), "ds0", 16)
        EI("gpsimd", lambda g: g.dma_start(rwt_bf, rwt_h[:]), "dma_c", 16)
        kt_bf = stream[:, 2048:6144]
        vt_bf = stream[:, 6144:10240]
        EI("gpsimd", lambda g: g.dma_start(kt_bf, wkt_h[:]), "dma_c", 16)
        EI("gpsimd", lambda g: g.dma_start(vt_bf, wvt_h[:]), "dma_c", 16)
        EI("gpsimd", lambda g: g.memset(neg12[:], NEGC))
        EI("gpsimd", lambda g: g.memset(onescb[:], 1.0))
        EI("gpsimd", lambda g: g.memset(onesrb[:], 1.0))
        EI("gpsimd", lambda g: g.memset(onesrf[:], 1.0))
        EI("gpsimd", lambda g: g.memset(wdT65[64:65, :], 1.0), "dma_c", 1, mark="consts")
        cdone = cnt["dma_c"]
        sdone = cnt["ds0"]
        for eng in q:
            W(eng, "dma_c", cdone)
            W(eng, "ds0", sdone)

        batches = _batches()
        NB = len(batches)

        rows_b = stream_f[0:1, 6144:6656]

        def phaseA_tile(ti, own, a_or_j):
            par = ti % 2
            bpar = ti % 3
            pofs = 512 * par
            bofs = 512 * par
            idx = a_or_j
            m_row = rows_sb[0:1, 256 * par:256 * par + 128]
            v_row = rows_sb[0:1, 256 * par + 128:256 * par + 256]
            rstd_row = rows_b[0:1, 256 * par:256 * par + 128]
            nmr_row = rows_b[0:1, 256 * par + 128:256 * par + 256]

            WM("gpsimd", ("xnmul", ti - 3))
            WM("gpsimd", ("sq", ti - 3))
            EI("gpsimd", lambda g: g.dma_start(xfm_buf[bpar], xfm_own_h[idx, :, :]),
               ["dg0", "dg1", "dma_c"][bpar], 16, mark=("xfm", ti))
            gv, gvv = marks[("xfm", ti)]
            W("scalar", gv, gvv)
            WM("scalar", ("sxxm", ti - 3))
            EI("scalar", lambda s: s.activation(xsq_buf[bpar], xfm_buf[bpar], AF.Square),
               "act", 1, mark=("sq", ti))
            W("tensor", gv, gvv)
            WM("tensor", ("tbcp", ti - 2))
            oap = PS[3][0:1, pofs:pofs + 128][:, None, :].broadcast_to([1, 4, 128])
            for b in range(8):
                EI("tensor", lambda t, b=b, oap=oap: t.matmul(
                    oap, onescb[:], xfm_buf[bpar][:, b * 512:(b + 1) * 512],
                    start=(b == 0), stop=(b == 7)),
                   **({"sem": "pe", "mark": ("sxm", ti)} if b == 7 else {}))
            WM("tensor", ("sq", ti))
            oap2 = PS[3][0:1, pofs + 128:pofs + 256][:, None, :].broadcast_to([1, 4, 128])
            for b in range(8):
                EI("tensor", lambda t, b=b, oap2=oap2: t.matmul(
                    oap2, onescb[:], xsq_buf[bpar][:, b * 512:(b + 1) * 512],
                    start=(b == 0), stop=(b == 7)),
                   **({"sem": "pe", "mark": ("sxxm", ti)} if b == 7 else {}))
            WM("vector", ("sxxm", ti))
            WM("vector", ("bcmmA", ti - 2))
            EI("vector", lambda v: v.tensor_copy(rows_sb[0:1, 256 * par:256 * par + 256],
                                                 PS[3][0:1, pofs:pofs + 256]),
               "vec", 1, mark=("rowsA", ti))
            EIG(lambda v: v.tensor_scalar(m_row, m_row, 1.0 / D, None, ALU.mult))
            EIG(lambda v: v.tensor_tensor(rstd_row, m_row, m_row, ALU.mult))
            EIG(lambda v: v.tensor_scalar(v_row, v_row, 1.0 / D, None, ALU.mult))
            EIG(lambda v: v.tensor_tensor(v_row, v_row, rstd_row, ALU.subtract))
            EIG(lambda v: v.tensor_scalar(v_row, v_row, EPS, None, ALU.add),
                mark=("statA", ti))
            WM("scalar", ("statA", ti))
            EI("scalar", lambda s: s.activation(nmr_row, v_row, AF.Ln), "act", 1)
            W("scalar", "act", cnt["act"])
            EI("scalar", lambda s: s.activation(rstd_row, nmr_row, AF.Exp, scale=-0.5),
               "act", 1, mark=("rstdA", ti))
            WM("vector", ("rstdA", ti))
            EI("vector", lambda v: v.scalar_tensor_tensor(nmr_row, m_row, -1.0, rstd_row,
                                                          ALU.mult, ALU.mult),
               "vec", 1, mark=("nmrA", ti))
            WM("tensor", ("nmrA", ti))
            WM("tensor", ("bcopyA", ti - 2))
            EI("tensor", lambda t: t.matmul(PS[2][:, bofs:bofs + 256], onesrf[:],
                                            rows_b[0:1, 256 * par:256 * par + 256],
                                            start=True, stop=True),
               "pe", 1, mark=("bcmmA", ti))
            EI("tensor", lambda t: t.matmul(PS[3][:, pofs + 256:pofs + 257], rstd_row,
                                            onesrf[0:1, 0:1], start=True, stop=True))
            EI("tensor", lambda t: t.matmul(PS[3][:, pofs + 257:pofs + 258], nmr_row,
                                            onesrf[0:1, 0:1], start=True, stop=True),
               "pe", 1, mark=("tbA", ti))
            WM("vector", ("bcmmA", ti))
            WM("vector", ("xnadd", ti - 2))
            EI("vector", lambda v: v.tensor_copy(b_sb[:, 256 * par:256 * par + 256],
                                                 PS[2][:, bofs:bofs + 256]),
               "vec", 1, mark=("bcopyA", ti))
            WM("vector", ("tbA", ti))
            EI("vector", lambda v: v.tensor_copy(rstd_all[:, idx:idx + 1],
                                                 PS[3][:, pofs + 256:pofs + 257]))
            EI("vector", lambda v: v.tensor_copy(nmr_all[:, idx:idx + 1],
                                                 PS[3][:, pofs + 257:pofs + 258]),
               "vec", 1, mark=("tbcp", ti))
            x3 = (xfm_buf[bpar]).rearrange("p (c m) -> p c m", m=128)
            br = b_sb[:, 256 * par:256 * par + 128][:, None, :].broadcast_to([128, 32, 128])
            bn = b_sb[:, 256 * par + 128:256 * par + 256][:, None, :].broadcast_to([128, 32, 128])
            o3 = xn_own[:].rearrange("p (ic t) -> p ic t", t=1024)[:, :, idx * 128:(idx + 1) * 128]
            EI("vector", lambda v, o3=o3, x3=x3, br=br: v.tensor_tensor(o3, x3, br, ALU.mult),
               "vec", 1, mark=("xnmul", ti))
            EI("vector", lambda v, o3=o3, bn=bn: v.tensor_tensor(o3, o3, bn, ALU.add),
               "vec", 1, mark=("xnadd", ti))

        for j in range(NJ):
            phaseA_tile(j, True, j)
        WM("tensor", ("xnadd", NJ - 1))
        for half in range(2):
            for ic in range(32):
                EI("tensor", lambda t, half=half, ic=ic: t.matmul(
                    PS[0][:, half * 512:half * 512 + 512],
                    kt_bf[:, ic * 128:(ic + 1) * 128],
                    xn_own[:, ic * 1024 + half * 512: ic * 1024 + half * 512 + 512],
                    start=(ic == 0), stop=(ic == 31)))
        for j in range(NJ):
            for ic in range(32):
                EI("tensor", lambda t, j=j, ic=ic: t.matmul(
                    PS[1][:, j * 128:(j + 1) * 128],
                    xn_own[:, ic * 1024 + j * 128: ic * 1024 + (j + 1) * 128],
                    vt_bf[:, ic * 128:(ic + 1) * 128],
                    start=(ic == 0), stop=False))
            EI("tensor", lambda t, j=j: t.matmul(
                PS[1][:, j * 128:(j + 1) * 128], onesrb[:], wvb[:],
                start=False, stop=True),
               **({"sem": "pe", "mark": "kvA"} if j == NJ - 1 else {}))
        WM("scalar", "kvA")
        EI("scalar", lambda s: s.add(kown_sb, PS[0][:], wkb[:]), "act", 1)
        EI("scalar", lambda s: s.activation(vown_sb, PS[1][:], AF.Copy),
           "act", 1, mark="kvAe")
        WM("gpsimd", "kvAe")
        EI("gpsimd", lambda g: g.dma_start(kown_d[:], kown_sb), "dma_c", 16)
        vout = vown_d[:].rearrange("(j p) h -> p j h", p=128)
        vin = vown_sb.rearrange("p (j h) -> p j h", h=128)
        EI("gpsimd", lambda g: g.dma_start(vout, vin), "dma_c", 16, mark="kvout")
        W("gpsimd", "dma_c", cnt["dma_c"])
        EI("gpsimd", lambda g: g.collective_compute(
            "AllGather", mybir.AluOpType.bypass, replica_groups=[list(range(NCORE))],
            ins=[kown_d.ap().opt()], outs=[kgath_d.ap().opt()]), "coll", 1)
        EI("gpsimd", lambda g: g.collective_compute(
            "AllGather", mybir.AluOpType.bypass, replica_groups=[list(range(NCORE))],
            ins=[vown_d.ap().opt()], outs=[vgath_d.ap().opt()]), "coll", 1)
        W("sync", "coll", 2)
        for t_ in range(NT):
            r_, j_ = t_ % 8, t_ // 8
            EI("sync", lambda e, t_=t_, r_=r_, j_=j_: e.dma_start(
                K_fm[:, t_ * 128:(t_ + 1) * 128],
                kgath_d[r_ * 128:(r_ + 1) * 128, j_ * 128:(j_ + 1) * 128]), "kvld", 16)
            EI("sync", lambda e, t_=t_, r_=r_, j_=j_: e.dma_start(
                V_tm[:, t_ * 128:(t_ + 1) * 128],
                vgath_d[r_ * 1024 + j_ * 128: r_ * 1024 + (j_ + 1) * 128, 0:128]),
               "kvld", 16)
        kvld_total = cnt["kvld"]
        DUMP("dbg_K", K_fm[:])
        DUMP("dbg_V", V_tm[:])
        DUMP("dbg_xn", xn_own[:])
        DUMP("dbg_rstd", rstd_all[:])
        DUMP("dbg_nmr", nmr_all[:])
        DUMP("dbg_sx", sx2[:])
        DUMP("dbg_sxx", sxx2[:])
        DUMP("dbg_mt", mt2[:])
        DUMP("dbg_vt", vt2[:])
        DUMP("dbg_lt", lt2[:])
        DUMP("dbg_bsb", b_sb)
        DUMP("dbg_rows", rows_sb)

        WM("tensor", ("tbcp", NJ - 1))
        WM("tensor", ("bcopyA", NJ - 1))
        for j in range(NJ):
            par = j % 2
            lofs = 512 * par
            WM("tensor", ("lgev", j - 2))
            for ic in range(32):
                EI("tensor", lambda t, ic=ic, j=j, lofs=lofs: t.matmul(
                    PS[3][:, lofs:lofs + NE],
                    xn_own[:, ic * 1024 + j * 128: ic * 1024 + (j + 1) * 128],
                    rwt_bf[:, ic * NE:(ic + 1) * NE],
                    start=(ic == 0), stop=(ic == 31)),
                   **({"sem": "pe", "mark": ("lg", j)} if ic == 31 else {}))
            WM("vector", ("lg", j))
            EI("vector", lambda v, j=j, lofs=lofs: v.scalar_tensor_tensor(
                logits_all[:, j * NE:(j + 1) * NE], PS[3][:, lofs:lofs + NE],
                1.0, rbb[:], ALU.mult, ALU.add),
               "vec", 1, mark=("lgev", j))
        WM("vector", ("lgev", NJ - 1))
        l3 = logits_all.rearrange("p (j e) -> p j e", e=NE)
        EIG(lambda v: v.reduce_max(wsum[:], l3, axis=AX.X))
        mxb = wsum[:][:, :, None].broadcast_to([128, NJ, NE])
        EIG(lambda v: v.tensor_tensor(l3, l3, mxb, ALU.subtract), mark="lgsub")
        WM("scalar", "lgsub")
        WM("scalar", "kvout")
        EI("scalar", lambda s: s.activation(s_all, logits_all, AF.Exp),
           "act", 1, mark="fexpw")

        for oc in range(32):
            par = oc % 2
            WM("gpsimd", ("qmm", oc - 2))
            if oc < 2:
                WM("gpsimd", ("lg", NJ - 1))
                WM("gpsimd", "kvA")
            EI("gpsimd", lambda g, oc=oc, par=par: g.dma_start(wq_slab[par], bigwq_h[oc, :, :]),
               f"dg{par}", 16, mark=("wqs", oc))
            sv, vv = marks[("wqs", oc)]
            W("tensor", sv, vv)
            if oc < 2:
                WM("tensor", "kvAe")
            WM("tensor", ("qevac", oc - 2))
            for ic in range(32):
                for half in range(2):
                    EI("tensor", lambda t, half=half, ic=ic, par=par: t.matmul(
                        PS[par][:, half * 512:half * 512 + 512],
                        wq_slab[par][:, ic * 128:(ic + 1) * 128],
                        xn_own[:, ic * 1024 + half * 512: ic * 1024 + half * 512 + 512],
                        start=(ic == 0), stop=(ic == 31)),
                       **({"sem": "pe", "mark": ("qmm", oc)} if (half == 1 and ic == 31) else {}))
            WM("scalar", ("qmm", oc))
            EI("scalar", lambda s, oc=oc, par=par: s.add(
                qpool[:, oc * 1024:(oc + 1) * 1024],
                PS[par][:], wqb[:, oc:oc + 1]),
               "act", 1, mark=("qevac", oc))
        s3 = s_all.rearrange("p (j e) -> p j e", e=NE)
        w3 = swork_w.rearrange("p (j e) -> p j e", e=NE)
        k3 = keep_w.rearrange("p (j e) -> p j e", e=NE)
        mskw = stream_f[:, 6144:6656]
        m3k = mskw.rearrange("p (j e) -> p j e", e=NE)
        wdense_w = logits_all
        WM("vector", "fexpw")
        EIG(lambda v: v.reduce_sum(ssum8[:], s3, axis=AX.X))
        EIG(lambda v: v.reciprocal(rsr8[:], ssum8[:]))
        rb3 = rsr8[:][:, :, None].broadcast_to([128, NJ, NE])
        EIG(lambda v: v.tensor_tensor(s3, s3, rb3, ALU.mult))
        EIG(lambda v: v.tensor_copy(swork_w, s_all))
        EIG(lambda v: v.memset(keep_w, 0.0))
        EIG(lambda v: v.memset(wsum[:], 0.0))
        for it in range(KTOP):
            EIG(lambda v: v.reduce_max(m8w[:], w3, axis=AX.X))
            EIG(lambda v: v.scalar_tensor_tensor(wsum[:], m8w[:], 0.0, wsum[:],
                                                 ALU.max, ALU.add))
            m3 = m8w[:][:, :, None].broadcast_to([128, NJ, NE])
            EIG(lambda v, m3=m3: v.tensor_tensor(m3k, w3, m3, ALU.is_ge))
            EIG(lambda v: v.tensor_tensor(k3, k3, m3k, ALU.add))
            EIG(lambda v: v.scalar_tensor_tensor(swork_w, mskw, -2.0, swork_w,
                                                 ALU.mult, ALU.add))
        EIG(lambda v: v.tensor_scalar(keep_w, keep_w, 1.0, None, ALU.is_ge))
        EIG(lambda v: v.tensor_tensor(wdense_w, s_all, keep_w, ALU.mult), mark="wdw")
        for j in range(NJ):
            tofs = 512 * (j % 2)
            WM("tensor", "wdw")
            WM("tensor", ("wdev", j - 2))
            EI("tensor", lambda t, j=j, tofs=tofs: t.transpose(
                PS[2][0:64, tofs:tofs + 128],
                wdense_w[:, j * 64:(j + 1) * 64], identf[:]),
               "pe", 1, mark=("wdT", j))
            WM("vector", ("wdT", j))
            EI("vector", lambda v, j=j, tofs=tofs: v.tensor_copy(
                wdT65[0:64, j * 128:(j + 1) * 128],
                PS[2][0:64, tofs:tofs + 128]),
               "vec", 1, mark=("wdev", j))
        EI("vector", lambda v: v.tensor_tensor(s1a[:], rstd_all[:], wsum[:], ALU.mult))
        EI("vector", lambda v: v.tensor_scalar(s1a[:], s1a[:], 1.0, None, ALU.add))
        EI("vector", lambda v: v.tensor_tensor(s2a[:], nmr_all[:], wsum[:], ALU.mult),
           "vec", 1, mark="epi_s")
        DUMP("dbg_wdt", wdT65[:])
        DUMP("dbg_wsum", wsum[:])
        DUMP("dbg_s1", s1a[:])
        DUMP("dbg_s2", s2a[:])
        WM("tensor", ("qevac", 31))
        WM("tensor", ("wdev", NJ - 1))
        WM("tensor", ("lgev", NJ - 1))
        WM("tensor", "epi_s")
        WM("scalar", "epi_s")
        WM("gpsimd", "epi_s")
        DUMP("dbg_q", qpool[:])

        def _bank_segs(a0, a1):
            segs = []
            for lo in range(0, 1024, 512):
                s0, s1 = max(a0, lo), min(a1, lo + 512)
                if s0 < s1:
                    segs.append((s0, s1))
            return segs

        def _prev_same_parity(b):
            return NB - 1 if (NB - 1) % 2 == b % 2 else NB - 2

        def S(h, b):
            chunks = batches[b]
            psb = PS[b % 2]
            key = ("exp", h, b - 2) if b >= 2 else ("exp", h - 1, _prev_same_parity(b))
            WM("tensor", key)
            for ci, (c, off, s) in enumerate(chunks):
                j0 = c // 8
                qbase = h * 1024 + j0 * 128
                segs = _bank_segs(off, off + s)
                for si, (s0, s1) in enumerate(segs):
                    last = (ci == len(chunks) - 1) and (si == len(segs) - 1)
                    EI("tensor", lambda t, c=c, off=off, s0=s0, s1=s1, qbase=qbase, psb=psb:
                       t.matmul(psb[:, s0:s1],
                                K_fm[:, c * 128:(c + 1) * 128],
                                qpool[:, qbase + s0 - off:qbase + s1 - off],
                                start=True, stop=True),
                       **({"sem": "pe", "mark": ("smm", h, b)} if last else {}))

        NEB = 7

        def _prev_same_modE(b):
            return NB - 1 - ((NB - 1 - b) % NEB)

        def EXP(h, b):
            chunks = batches[b]
            cols = chunks[-1][1] + chunks[-1][2]
            psb = PS[b % 2]
            eb = ebuf[b % NEB]
            key = ("vd", h, b - NEB) if b >= NEB else ("vd", h - 1, _prev_same_modE(b))
            WM("scalar", key)
            WM("scalar", ("smm", h, b))
            EI("scalar", lambda s, cols=cols, psb=psb, eb=eb:
               s.activation(eb[:, 0:cols], psb[:, 0:cols], AF.Exp, bias=neg12[:], scale=SCL),
               "act", 1, mark=("exp", h, b))

        def MASK(h, b):
            chunks = batches[b]
            eb = ebuf[b % NEB]
            WM("gpsimd", ("exp", h, b))
            n = len(chunks)
            stride = chunks[0][2]
            r0 = chunks[0][0] % 8
            assert all(s == stride for _, _, s in chunks)
            assert all((c % 8) == r0 + i for i, (c, _, _) in enumerate(chunks))
            e3 = eb[:, 0:stride * n].rearrange("p (ci m) -> p ci m", m=stride)[:, :, 0:128]
            m3 = mask8[:, r0 * 128:(r0 + n) * 128].rearrange("p (ci m) -> p ci m", m=128)
            EI("gpsimd", lambda g, e3=e3, m3=m3: g.tensor_tensor(e3, e3, m3, ALU.mult),
               "pool", 1, mark=("msk", h, b))

        def VD(h, b):
            chunks = batches[b]
            eb = ebuf[b % NEB]
            WM("tensor", ("msk", h, b))
            if b == 0:
                WM("tensor", ("oevac", h - 1))
            first = b == 0
            for ci, (c, off, s) in enumerate(chunks):
                j0 = c // 8
                base = j0 * 128
                segs = _bank_segs(base, 1024)
                for si, (s0, s1) in enumerate(segs):
                    st_ = first and ci == 0
                    e0 = off + s0 - base
                    e1 = off + s1 - base
                    EI("tensor", lambda t, c=c, s0=s0, s1=s1, e0=e0, e1=e1, st_=st_, eb=eb:
                       t.matmul(PS[2][:, s0:s1],
                                V_tm[:, c * 128:(c + 1) * 128],
                                eb[:, e0:e1],
                                start=st_, stop=False))
            nc_ = len(chunks)
            s = chunks[0][2]
            base = (chunks[0][0] // 8) * 128
            st_ = b == 0
            if s > 512:
                assert nc_ == 1
                segs = _bank_segs(base, base + s) if base + s <= 1024 else None
                segs = _bank_segs(base, 1024)
                for si, (s0, s1) in enumerate(segs):
                    EI("tensor", lambda t, s0=s0, s1=s1, base=base, st_=st_, eb=eb:
                       t.matmul(PS[3][0:1, s0:s1], onescb[:],
                                eb[:, s0 - base:s1 - base], start=st_, stop=False),
                       **({"sem": "pe", "mark": ("vd", h, b)} if si == len(segs) - 1 else {}))
            else:
                step = max(1, 512 // s)
                for g0 in range(0, nc_, step):
                    g1 = min(g0 + step, nc_)
                    last = g1 == nc_
                    if g1 - g0 == 1:
                        oap = PS[3][0:1, base:base + s]
                    else:
                        oap = PS[3][0:1, base:base + s][:, None, :].broadcast_to(
                            [1, g1 - g0, s])
                    EI("tensor", lambda t, oap=oap, g0=g0, g1=g1, s=s, st_=st_, eb=eb:
                       t.matmul(oap, onescb[:], eb[:, g0 * s:g1 * s], start=st_, stop=False),
                       **({"sem": "pe", "mark": ("vd", h, b)} if last else {}))

        def RECIP(h):
            WM("scalar", ("vd", h, NB - 1))
            WM("scalar", ("bcmm", h - 1))
            EI("scalar", lambda s: s.activation(lnrow, PS[3][0:1, 0:1024], AF.Ln), "act", 1)
            W("scalar", "act", cnt["act"])
            EI("scalar", lambda s: s.activation(rec, lnrow, AF.Exp, scale=-1.0),
               "act", 1, mark=("recip", h))

        def ENDGAME(h):
            WM("tensor", ("recip", h))
            EI("tensor", lambda t: t.matmul(PS[3][:, 0:512], onesrf[:], rec[0:1, 0:512],
                                            start=True, stop=True))
            EI("tensor", lambda t: t.matmul(PS[3][:, 512:1024], onesrf[:], rec[0:1, 512:1024],
                                            start=True, stop=True),
               "pe", 1, mark=("bcmm", h))
            WM("vector", ("bcmm", h))
            EI("vector", lambda v: v.tensor_copy(bsb, PS[3][:]))
            EI("vector", lambda v, h=h: v.tensor_tensor(qpool[:, h * 1024:(h + 1) * 1024],
                                                        PS[2][:], bsb, ALU.mult),
               "vec", 1, mark=("oevac", h))

        W("tensor", "kvld", kvld_total)
        for h in range(H):
            S(h, 0)
            S(h, 1)
            if h > 0:
                ENDGAME(h - 1)
            EXP(h, 0)
            MASK(h, 0)
            for b in range(2, NB):
                S(h, b)
                if b >= NEB:
                    VD(h, b - NEB)
                EXP(h, b - 1)
                MASK(h, b - 1)
            EXP(h, NB - 1)
            MASK(h, NB - 1)
            for b in range(NB - NEB, NB):
                VD(h, b)
            RECIP(h)
        ENDGAME(H - 1)
        barrier()
        DUMP("dbg_o", qpool[:])

        EI("gpsimd", lambda g: g.dma_start(eb65, eb65_h[:]), "dma_c", 16, mark="eb65ld")
        sv, vv = marks["eb65ld"]
        W("tensor", sv, vv)
        nstores = 0
        for p in range(2):
            for j in range(NJ):
                ei = p * NJ + j
                par = ei % 2
                WM("sync", ("t1", ei - 2))
                EI("sync", lambda e, p=p, j=j, par=par: e.dma_start(
                    xbufE[par], xtm_own_h[j, :, p * 2048:p * 2048 + 2048]),
                   f"ds{par}", 16, mark=("xE", ei))
                sv, vv = marks[("xE", ei)]
                W("vector", sv, vv)
                if p == 1 and j == 0:
                    WM("vector", ("store", 7))
                EI("vector", lambda v, j=j, par=par: v.tensor_scalar(
                    stg[j], xbufE[par], s1a[:, j:j + 1], s2a[:, j:j + 1], ALU.mult, ALU.add),
                   "vec", 1, mark=("t1", ei))
                if debug and p == 1 and j == 0:
                    sv_, vv_ = marks[("t1", ei)]
                    W("gpsimd", sv_, vv_)
                    DUMP("dbg_e1", stg[0])
                    DUMP("dbg_s1b", s1a[:])
                    W("vector", "st", cnt["st"])
            for dg in range(8):
                dmg = p * 8 + dg
                par = dmg % 2
                pspair = (PS[0], PS[1]) if par == 0 else (PS[2], PS[3])
                WM("gpsimd", ("pp", dmg - 2))
                EI("gpsimd", lambda g, dmg=dmg, par=par: g.dma_start(
                    wo_slab[par], bigwo_h[dmg, :, :]),
                   f"dg{par}", 16, mark=("wos", dmg))
                sv, vv = marks[("wos", dmg)]
                W("tensor", sv, vv)
                WM("tensor", ("ppev", dmg - 2))
                for j in range(NJ):
                    psj = pspair[0] if j < 4 else pspair[1]
                    jo = (j % 4) * 256
                    for hh in range(H):
                        EI("tensor", lambda t, j=j, hh=hh, psj=psj, jo=jo, par=par: t.matmul(
                            psj[:, jo:jo + 256],
                            qpool[:, hh * 1024 + j * 128: hh * 1024 + (j + 1) * 128],
                            wo_slab[par][:, hh * 256:(hh + 1) * 256],
                            start=(hh == 0), stop=False))
                    EI("tensor", lambda t, j=j, psj=psj, jo=jo, dmg=dmg: t.matmul(
                        psj[:, jo:jo + 256],
                        wdT65[:, j * 128:(j + 1) * 128],
                        eb65[:, dmg * 256:(dmg + 1) * 256],
                        start=False, stop=True),
                       **({"sem": "pe", "mark": ("pp", dmg)} if j == NJ - 1 else {}))
                WM("vector", ("pp", dmg))
                for j in range(NJ):
                    psj = pspair[0] if j < 4 else pspair[1]
                    jo = (j % 4) * 256
                    kw = {"sem": "vec", "mark": ("ppev", dmg)} if j == NJ - 1 else                          ({"sem": "vec", "mark": ("ppevj", dmg, j)} if dg == 7 else {})
                    EI("vector", lambda v, j=j, dg=dg, psj=psj, jo=jo: v.tensor_tensor(
                        stg[j][:, dg * 256:(dg + 1) * 256],
                        stg[j][:, dg * 256:(dg + 1) * 256],
                        psj[:, jo:jo + 256], ALU.add), **kw)
            if debug and p == 1:
                sv, vv = marks[("ppev", p * 8 + 7)]
                W("gpsimd", sv, vv)
                DUMP("dbg_e2", stg[0])
                W("sync", "st", cnt["st"])
            for j in range(NJ):
                key = ("ppevj", p * 8 + 7, j) if j < NJ - 1 else ("ppev", p * 8 + 7)
                sv, vv = marks[key]
                W("sync", sv, vv)
                EI("sync", lambda e, p=p, j=j: e.dma_start(
                    out_h[j, :, p * 2048:p * 2048 + 2048], stg[j]),
                   "st", 16, mark=("store", nstores))
                nstores += 1
        W("sync", "st", cnt["st"])

        ninstr = sum(len(v) for v in q.values())
        print(f"[kernel build] instructions: {ninstr} "
              f"({ {k: len(v) for k, v in q.items()} })")

        block = stack.enter_context(nc.Block())

        @block.sync
        def _(e):
            for f in q["sync"]:
                f(e)

        @block.gpsimd
        def _(e):
            for f in q["gpsimd"]:
                f(e)

        @block.tensor
        def _(e):
            for f in q["tensor"]:
                f(e)

        @block.scalar
        def _(e):
            for f in q["scalar"]:
                f(e)

        @block.vector
        def _(e):
            for f in q["vector"]:
                f(e)

    return nc


def _prep_inputs(L0_attn_x, L0_attn_wq_w, L0_attn_wq_b, L0_attn_wk_w, L0_attn_wk_b,
                 L0_attn_wv_w, L0_attn_wv_b, L0_attn_wo_w, L0_attn_wo_b,
                 L0_ffn_router_weight, L0_ffn_router_bias, L0_ffn_expert_bias):
    f32 = np.float32
    x = np.ascontiguousarray(np.asarray(L0_attn_x, dtype=f32).reshape(T, D))
    wq = np.asarray(L0_attn_wq_w, dtype=f32)
    wk = np.asarray(L0_attn_wk_w, dtype=f32)
    wv = np.asarray(L0_attn_wv_w, dtype=f32)
    wo = np.asarray(L0_attn_wo_w, dtype=f32)
    rw = np.asarray(L0_ffn_router_weight, dtype=f32)
    eb = np.asarray(L0_ffn_expert_bias, dtype=f32)
    wqb_ = np.asarray(L0_attn_wq_b, dtype=f32)
    wkb_ = np.asarray(L0_attn_wk_b, dtype=f32)
    wvb_ = np.asarray(L0_attn_wv_b, dtype=f32)
    wob_ = np.asarray(L0_attn_wo_b, dtype=f32)
    rbb_ = np.asarray(L0_ffn_router_bias, dtype=f32)

    xT = np.ascontiguousarray(x.T)
    xfm_all = np.ascontiguousarray(
        xT.reshape(32, 128, NT, 128).transpose(2, 1, 0, 3).reshape(NT, 128, D))
    xtm_all = np.ascontiguousarray(x.reshape(NT, 128, D))
    wqT = np.ascontiguousarray(wq.T)
    bigwq = np.ascontiguousarray(
        wqT.reshape(32, 128, 32, 128).transpose(2, 1, 0, 3).reshape(32, 128, D))
    woT = np.ascontiguousarray(wo.T)
    bigwo = np.ascontiguousarray(
        woT.reshape(32, 128, 16, 256).transpose(2, 1, 0, 3).reshape(16, 128, 8192))
    wkt = np.ascontiguousarray(wk.T.reshape(32, 128, 128).transpose(1, 0, 2).reshape(128, D))
    wvt = np.ascontiguousarray(wv.T.reshape(32, 128, 128).transpose(1, 0, 2).reshape(128, D))
    rwt = np.ascontiguousarray(rw.T.reshape(32, 128, NE).transpose(1, 0, 2).reshape(128, 2048))
    eb65 = np.ascontiguousarray(np.concatenate([eb, wob_[None, :]], axis=0))
    wqb_in = np.ascontiguousarray(wqb_.reshape(32, 128).T)
    wkb_in = np.ascontiguousarray(wkb_.reshape(128, 1))
    wvb_in = np.ascontiguousarray(wvb_.reshape(1, 128))
    rbb_in = np.ascontiguousarray(np.broadcast_to(rbb_[None, :], (128, NE)))
    identf = np.eye(128, dtype=f32)

    tril = (np.arange(128)[:, None] <= np.arange(128)[None, :]).astype(f32)
    onesb = np.ones((128, 128), dtype=f32)
    zerosb = np.zeros((128, 128), dtype=f32)

    shared = dict(bigwq=bigwq, bigwo=bigwo,
                  wkt=wkt, wvt=wvt, rwt=rwt, eb65=eb65, wqb=wqb_in, wkb=wkb_in,
                  wvb=wvb_in, rbb=rbb_in, identf=identf)
    in_maps = []
    for i in range(NCORE):
        own = [8 * j + i for j in range(NJ)]
        m8 = np.concatenate(
            [onesb if r < i else (tril if r == i else zerosb) for r in range(8)],
            axis=1).astype(ml_dtypes.bfloat16)
        d = dict(shared)
        d["xfm_own"] = np.ascontiguousarray(xfm_all[own])
        d["xtm_own"] = np.ascontiguousarray(xtm_all[own])
        d["mask8"] = m8
        in_maps.append(d)
    return in_maps


def kernel(**inputs):
    in_maps = _prep_inputs(**inputs)
    nc = build_nc()
    res = run_bass_kernel_spmd(nc, in_maps, core_ids=list(range(NCORE)))
    full = np.empty((T, D), dtype=np.float32)
    for i, r in enumerate(res.results):
        o = r["out"]
        for j in range(NJ):
            t = 8 * j + i
            full[t * 128:(t + 1) * 128] = o[j]
    return full.reshape(1, T, D)


# version 19
# speedup vs baseline: 1.0021x; 1.0021x over previous
import math
from collections import defaultdict
from contextlib import ExitStack

import numpy as np
import ml_dtypes

import concourse.bass as bass
import concourse.mybir as mybir
from concourse.bass_utils import run_bass_kernel_spmd

F32 = mybir.dt.float32
BF16 = mybir.dt.bfloat16
AF = mybir.ActivationFunctionType
ALU = mybir.AluOpType
AX = mybir.AxisListType

D = 4096
T = 8192
H = 32
HD = 128
NT = 64
NJ = 8
NCORE = 8
NE = 64
KTOP = 8
SCL = 1.0 / math.sqrt(128.0)
EPS = 1e-5
NEGC = -12.0


def _batches():
    sizes = [(8 - c // 8) * 128 for c in range(64)]
    batches, cur, cols = [], [], 0
    for c, s in enumerate(sizes):
        if cols + s > 1024 or (cur and s != cur[0][2]):
            batches.append(cur)
            cur, cols = [], 0
        cur.append((c, cols, s))
        cols += s
    batches.append(cur)
    return batches


def build_nc(debug=False):
    nc = bass.Bass("TRN2", target_bir_lowering=False, debug=False)
    dr = lambda n, s, dt=F32, kind="ExternalInput": nc.dram_tensor(n, s, dt, kind=kind)

    xfm_own_h = dr("xfm_own", [NJ, 128, D])
    xtm_own_h = dr("xtm_own", [NJ, 128, D])
    bigwq_h = dr("bigwq", [32, 128, D])
    bigwo_h = dr("bigwo", [16, 128, 8192])
    wkt_h = dr("wkt", [128, D])
    wvt_h = dr("wvt", [128, D])
    rwt_h = dr("rwt", [128, 2048])
    eb65_h = dr("eb65", [65, D])
    wqb_h = dr("wqb", [128, 32])
    wkb_h = dr("wkb", [128, 1])
    wvb_h = dr("wvb", [1, 128])
    rbb_h = dr("rbb", [128, NE])
    identf_h = dr("identf", [128, 128])
    mask8_h = dr("mask8", [128, 1024], BF16)
    out_h = dr("out", [NJ, 128, D], F32, kind="ExternalOutput")
    kown_d = nc.dram_tensor("kown_b", [128, 1024], BF16)
    kgath_d = nc.dram_tensor("kgath_b", [1024, 1024], BF16, addr_space="Shared")
    vown_d = nc.dram_tensor("vown_b", [1024, 128], BF16)
    vgath_d = nc.dram_tensor("vgath_b", [T, 128], BF16, addr_space="Shared")
    dbg_h = {}
    if debug:
        for n, s, dt in [("dbg_rstd", [128, NJ], F32), ("dbg_nmr", [128, NJ], F32),
                         ("dbg_xn", [128, 32768], BF16), ("dbg_K", [128, T], BF16),
                         ("dbg_V", [128, T], BF16), ("dbg_q", [128, 32768], BF16),
                         ("dbg_o", [128, 32768], BF16), ("dbg_wdt", [65, 1024], BF16),
                         ("dbg_wsum", [128, NJ], F32), ("dbg_s1", [128, NJ], F32),
                         ("dbg_s2", [128, NJ], F32), ("dbg_sx", [128, 2], F32),
                         ("dbg_sxx", [128, 2], F32), ("dbg_mt", [128, 2], F32),
                         ("dbg_vt", [128, 2], F32), ("dbg_lt", [128, 2], F32),
                         ("dbg_bsb", [128, 512], F32), ("dbg_rows", [1, 512], F32),
                         ("dbg_lg", [128, 512], F32), ("dbg_snap", [128, 512], F32),
                         ("dbg_nm8", [128, NJ], F32), ("dbg_ss8", [128, NJ], F32),
                         ("dbg_rs8", [128, NJ], F32), ("dbg_e1", [128, 2048], F32),
                         ("dbg_e2", [128, 2048], F32), ("dbg_e3", [128, 256], F32),
                         ("dbg_s1b", [128, NJ], F32)]:
            dbg_h[n] = dr(n, s, dt, kind="ExternalOutput")

    with ExitStack() as stack:
        sb = lambda n, s, dt: stack.enter_context(nc.sbuf_tensor(n, s, dt))
        psa = lambda n: stack.enter_context(nc.psum_tensor(n, [128, 1024], F32))
        mksem = lambda n: stack.enter_context(nc.semaphore(n))

        identf = sb("identf_s", [128, 128], F32)
        mask8 = sb("mask8_s", [128, 1024], BF16)
        wqb = sb("wqb_s", [128, 32], F32)
        wkb = sb("wkb_s", [128, 1], F32)
        wvb = sb("wvb_s", [1, 128], BF16)
        rbb = sb("rbb_s", [128, NE], F32)
        wdT65 = sb("wdT65_s", [65, 1024], BF16)
        neg12 = sb("neg12_s", [128, 1], F32)
        onescb = sb("onescb_s", [128, 1], BF16)
        onesrb = sb("onesrb_s", [1, 128], BF16)
        onesrf = sb("onesrf_s", [1, 128], F32)
        rstd_all = sb("rstd_all_s", [128, NJ], F32)
        nmr_all = sb("nmr_all_s", [128, NJ], F32)
        wsum = sb("wsum_s", [128, NJ], F32)
        s1a = sb("s1a_s", [128, NJ], F32)
        s2a = sb("s2a_s", [128, NJ], F32)
        sx2 = sb("sx2_s", [128, 2], F32)
        sxx2 = sb("sxx2_s", [128, 2], F32)
        mt2 = sb("mt2_s", [128, 2], F32)
        mq2 = sb("mq2_s", [128, 2], F32)
        vt2 = sb("vt2_s", [128, 2], F32)
        lt2 = sb("lt2_s", [128, 2], F32)
        rst2 = sb("rst2_s", [128, 2], F32)
        nmr2 = sb("nmr2_s", [128, 2], F32)
        logits_sb = sb("logits_s", [128, NE], F32)
        if debug:
            dbg_lg_sb = sb("dbg_lg_sb", [128, 512], F32)
            dbg_snap_sb = sb("dbg_snap_sb", [128, 512], F32)
            dbg_nm8_sb = sb("dbg_nm8_sb", [128, NJ], F32)
            dbg_ss8_sb = sb("dbg_ss8_sb", [128, NJ], F32)
            dbg_rs8_sb = sb("dbg_rs8_sb", [128, NJ], F32)
        s_sb = sb("ssm_s", [128, NE], F32)
        swork = sb("swork_s", [128, NE], F32)
        keep = sb("keep_s", [128, NE], F32)
        mske = sb("mske_s", [128, NE], F32)
        wdense = sb("wdense_s", [128, NE], F32)
        m8 = sb("m8_s", [128, 1], F32)
        ssum8 = sb("ssum8_s", [128, NJ], F32)
        rsr8 = sb("rsr8_s", [128, NJ], F32)
        m8w = sb("m8w_s", [128, NJ], F32)
        nmax = sb("nmax_s", [128, 1], F32)
        ssum = sb("ssum_s", [128, 1], F32)
        rsr = sb("rsr_s", [128, 1], F32)

        K_fm = sb("K_fm", [128, T], BF16)
        V_tm = sb("V_tm", [128, T], BF16)
        qpool = sb("qpool", [128, 32768], BF16)
        xn_own = sb("xn_own", [128, 32768], BF16)
        stream = sb("stream", [128, 16384], BF16)

        qpool_f = qpool.bitcast(F32)
        xn_f = xn_own.bitcast(F32)
        stream_f = stream.bitcast(F32)
        kv_f = K_fm.bitcast(F32)

        assert list(qpool_f.shape) == [128, 16384], qpool_f.shape

        xfm_buf = [qpool[:, k * 4096:(k + 1) * 4096] for k in range(3)]
        xsq_buf = [qpool[:, 12288 + k * 4096:12288 + (k + 1) * 4096] for k in range(3)]
        rwt_bf = stream[:, 0:2048]
        wq_slab = [stream[:, 0:4096], stream[:, 4096:8192]]
        ebuf = [stream[:, 8192:9216], stream[:, 9216:10240], stream[:, 0:1024],
                stream[:, 1024:2048], stream[:, 2048:3072], stream[:, 3072:4096],
                stream[:, 4096:5120]]
        kown_sb = stream[:, 8192:9216]
        vown_sb = stream[:, 9216:10240]
        bsb = stream_f[:, 5120:6144]
        rec = stream_f[0:1, 6144:7168]
        lnrow = stream_f[0:1, 7168:8192]
        wo_slab = [stream[:, 0:8192], stream[:, 8192:16384]]
        stg = [xn_f[:, j * 2048:(j + 1) * 2048] for j in range(NJ)]
        xbufE = [kv_f[:, 0:2048], kv_f[:, 2048:4096]]
        eb65 = V_tm[0:65, 0:4096]
        b_sb = stream_f[:, 5120:5632]
        rows_sb = stream_f[0:1, 5632:6144]
        logits_all = stream_f[:, 5632:6144]
        s_all = stream_f[:, 4096:4608]
        swork_w = stream_f[:, 4608:5120]
        keep_w = stream_f[:, 5120:5632]

        PS = [psa(f"PS{i}") for i in range(4)]

        sems = {n: mksem(n) for n in ["ds0", "ds1", "dg0", "dg1", "dma_c", "st",
                                      "pe", "act", "vec", "pool", "coll", "kvld"]}
        cnt = defaultdict(int)
        marks = {}
        q = {"sync": [], "gpsimd": [], "tensor": [], "scalar": [], "vector": []}

        def EI(eng, bf, sem=None, k=1, mark=None):
            if eng == "vector" and sem is None:
                sem = "vec"
            if sem is not None:
                cnt[sem] += k
                if mark is not None:
                    marks[mark] = (sem, cnt[sem])
                sh = sems[sem]

                def g(e, bf=bf, sh=sh, k=k):
                    bf(e).then_inc(sh, k)
            else:
                def g(e, bf=bf):
                    bf(e)
            q[eng].append(g)
            if eng == "vector":
                W("vector", "vec", cnt["vec"])

        def W(eng, sem, v):
            if v > 0:
                q[eng].append(lambda e, sh=sems[sem], v=v: e.wait_ge(sh, v))

        def WM(eng, key):
            if key in marks:
                s, v = marks[key]
                W(eng, s, v)

        def EIG(bf, mark=None):
            EI("vector", bf, "vec", 1, mark=mark)

        def barrier():
            snap = [(s, cnt[s]) for s in sems if s not in ("coll", "kvld")]
            for eng in q:
                for s, v in snap:
                    W(eng, s, v)

        def DUMP(name, ap):
            if debug:
                EI("gpsimd", lambda g, ap=ap, h=dbg_h[name]: g.dma_start(h[:], ap), "st", 16)

        EI("sync", lambda e: e.dma_start(identf[:], identf_h[:]), "ds0", 16)
        EI("sync", lambda e: e.dma_start(mask8[:], mask8_h[:]), "ds0", 16)
        EI("sync", lambda e: e.dma_start(wqb[:], wqb_h[:]), "ds0", 16)
        EI("sync", lambda e: e.dma_start(wkb[:], wkb_h[:]), "ds0", 16)
        EI("gpsimd", lambda g: g.dma_start(wvb[:], wvb_h[:]), "dma_c", 16)
        EI("sync", lambda e: e.dma_start(rbb[:], rbb_h[:]), "ds0", 16)
        EI("gpsimd", lambda g: g.dma_start(rwt_bf, rwt_h[:]), "dma_c", 16)
        kt_bf = stream[:, 2048:6144]
        vt_bf = stream[:, 6144:10240]
        EI("gpsimd", lambda g: g.dma_start(kt_bf, wkt_h[:]), "dma_c", 16)
        EI("gpsimd", lambda g: g.dma_start(vt_bf, wvt_h[:]), "dma_c", 16)
        EI("gpsimd", lambda g: g.memset(neg12[:], NEGC))
        EI("gpsimd", lambda g: g.memset(onescb[:], 1.0))
        EI("gpsimd", lambda g: g.memset(onesrb[:], 1.0))
        EI("gpsimd", lambda g: g.memset(onesrf[:], 1.0))
        EI("gpsimd", lambda g: g.memset(wdT65[64:65, :], 1.0), "dma_c", 1, mark="consts")
        cdone = cnt["dma_c"]
        sdone = cnt["ds0"]
        for eng in q:
            W(eng, "dma_c", cdone)
            W(eng, "ds0", sdone)

        batches = _batches()
        NB = len(batches)

        def WARM(n):
            for _ in range(n):
                EI("tensor", lambda t: t.matmul(PS[0][:, 0:512], mask8[:, 0:128],
                                                mask8[:, 0:512], start=True, stop=True))

        rows_b = stream_f[0:1, 6144:6656]

        def phaseA_tile(ti, own, a_or_j):
            par = ti % 2
            bpar = ti % 3
            pofs = 512 * par
            bofs = 512 * par
            idx = a_or_j
            m_row = rows_sb[0:1, 256 * par:256 * par + 128]
            v_row = rows_sb[0:1, 256 * par + 128:256 * par + 256]
            rstd_row = rows_b[0:1, 256 * par:256 * par + 128]
            nmr_row = rows_b[0:1, 256 * par + 128:256 * par + 256]

            WM("gpsimd", ("xnmul", ti - 3))
            WM("gpsimd", ("sq", ti - 3))
            EI("gpsimd", lambda g: g.dma_start(xfm_buf[bpar], xfm_own_h[idx, :, :]),
               ["dg0", "dg1", "dma_c"][bpar], 16, mark=("xfm", ti))
            gv, gvv = marks[("xfm", ti)]
            W("scalar", gv, gvv)
            WM("scalar", ("sxxm", ti - 3))
            EI("scalar", lambda s: s.activation(xsq_buf[bpar], xfm_buf[bpar], AF.Square),
               "act", 1, mark=("sq", ti))
            W("tensor", gv, gvv)
            WM("tensor", ("tbcp", ti - 2))
            oap = PS[3][0:1, pofs:pofs + 128][:, None, :].broadcast_to([1, 4, 128])
            for b in range(8):
                EI("tensor", lambda t, b=b, oap=oap: t.matmul(
                    oap, onescb[:], xfm_buf[bpar][:, b * 512:(b + 1) * 512],
                    start=(b == 0), stop=(b == 7)),
                   **({"sem": "pe", "mark": ("sxm", ti)} if b == 7 else {}))
            WM("tensor", ("sq", ti))
            oap2 = PS[3][0:1, pofs + 128:pofs + 256][:, None, :].broadcast_to([1, 4, 128])
            for b in range(8):
                EI("tensor", lambda t, b=b, oap2=oap2: t.matmul(
                    oap2, onescb[:], xsq_buf[bpar][:, b * 512:(b + 1) * 512],
                    start=(b == 0), stop=(b == 7)),
                   **({"sem": "pe", "mark": ("sxxm", ti)} if b == 7 else {}))
            WM("vector", ("sxxm", ti))
            WM("vector", ("bcmmA", ti - 2))
            EI("vector", lambda v: v.tensor_copy(rows_sb[0:1, 256 * par:256 * par + 256],
                                                 PS[3][0:1, pofs:pofs + 256]),
               "vec", 1, mark=("rowsA", ti))
            EIG(lambda v: v.tensor_scalar(m_row, m_row, 1.0 / D, None, ALU.mult))
            EIG(lambda v: v.tensor_tensor(rstd_row, m_row, m_row, ALU.mult))
            EIG(lambda v: v.tensor_scalar(v_row, v_row, 1.0 / D, None, ALU.mult))
            EIG(lambda v: v.tensor_tensor(v_row, v_row, rstd_row, ALU.subtract))
            EIG(lambda v: v.tensor_scalar(v_row, v_row, EPS, None, ALU.add),
                mark=("statA", ti))
            WM("scalar", ("statA", ti))
            EI("scalar", lambda s: s.activation(nmr_row, v_row, AF.Ln), "act", 1)
            W("scalar", "act", cnt["act"])
            EI("scalar", lambda s: s.activation(rstd_row, nmr_row, AF.Exp, scale=-0.5),
               "act", 1, mark=("rstdA", ti))
            WM("vector", ("rstdA", ti))
            EI("vector", lambda v: v.scalar_tensor_tensor(nmr_row, m_row, -1.0, rstd_row,
                                                          ALU.mult, ALU.mult),
               "vec", 1, mark=("nmrA", ti))
            WM("tensor", ("nmrA", ti))
            WM("tensor", ("bcopyA", ti - 2))
            EI("tensor", lambda t: t.matmul(PS[2][:, bofs:bofs + 256], onesrf[:],
                                            rows_b[0:1, 256 * par:256 * par + 256],
                                            start=True, stop=True),
               "pe", 1, mark=("bcmmA", ti))
            EI("tensor", lambda t: t.matmul(PS[3][:, pofs + 256:pofs + 257], rstd_row,
                                            onesrf[0:1, 0:1], start=True, stop=True))
            EI("tensor", lambda t: t.matmul(PS[3][:, pofs + 257:pofs + 258], nmr_row,
                                            onesrf[0:1, 0:1], start=True, stop=True),
               "pe", 1, mark=("tbA", ti))
            WM("vector", ("bcmmA", ti))
            WM("vector", ("xnadd", ti - 2))
            EI("vector", lambda v: v.tensor_copy(b_sb[:, 256 * par:256 * par + 256],
                                                 PS[2][:, bofs:bofs + 256]),
               "vec", 1, mark=("bcopyA", ti))
            WM("vector", ("tbA", ti))
            EI("vector", lambda v: v.tensor_copy(rstd_all[:, idx:idx + 1],
                                                 PS[3][:, pofs + 256:pofs + 257]))
            EI("vector", lambda v: v.tensor_copy(nmr_all[:, idx:idx + 1],
                                                 PS[3][:, pofs + 257:pofs + 258]),
               "vec", 1, mark=("tbcp", ti))
            x3 = (xfm_buf[bpar]).rearrange("p (c m) -> p c m", m=128)
            br = b_sb[:, 256 * par:256 * par + 128][:, None, :].broadcast_to([128, 32, 128])
            bn = b_sb[:, 256 * par + 128:256 * par + 256][:, None, :].broadcast_to([128, 32, 128])
            o3 = xn_own[:].rearrange("p (ic t) -> p ic t", t=1024)[:, :, idx * 128:(idx + 1) * 128]
            EI("vector", lambda v, o3=o3, x3=x3, br=br: v.tensor_tensor(o3, x3, br, ALU.mult),
               "vec", 1, mark=("xnmul", ti))
            EI("vector", lambda v, o3=o3, bn=bn: v.tensor_tensor(o3, o3, bn, ALU.add),
               "vec", 1, mark=("xnadd", ti))

        WARM(16)
        for j in range(NJ):
            phaseA_tile(j, True, j)
            WARM(10)
        WM("tensor", ("xnadd", NJ - 1))
        for half in range(2):
            for ic in range(32):
                EI("tensor", lambda t, half=half, ic=ic: t.matmul(
                    PS[0][:, half * 512:half * 512 + 512],
                    kt_bf[:, ic * 128:(ic + 1) * 128],
                    xn_own[:, ic * 1024 + half * 512: ic * 1024 + half * 512 + 512],
                    start=(ic == 0), stop=(ic == 31)))
        for j in range(NJ):
            for ic in range(32):
                EI("tensor", lambda t, j=j, ic=ic: t.matmul(
                    PS[1][:, j * 128:(j + 1) * 128],
                    xn_own[:, ic * 1024 + j * 128: ic * 1024 + (j + 1) * 128],
                    vt_bf[:, ic * 128:(ic + 1) * 128],
                    start=(ic == 0), stop=False))
            EI("tensor", lambda t, j=j: t.matmul(
                PS[1][:, j * 128:(j + 1) * 128], onesrb[:], wvb[:],
                start=False, stop=True),
               **({"sem": "pe", "mark": "kvA"} if j == NJ - 1 else {}))
        WM("scalar", "kvA")
        EI("scalar", lambda s: s.add(kown_sb, PS[0][:], wkb[:]), "act", 1)
        EI("scalar", lambda s: s.activation(vown_sb, PS[1][:], AF.Copy),
           "act", 1, mark="kvAe")
        WM("gpsimd", "kvAe")
        EI("gpsimd", lambda g: g.dma_start(kown_d[:], kown_sb), "dma_c", 16)
        vout = vown_d[:].rearrange("(j p) h -> p j h", p=128)
        vin = vown_sb.rearrange("p (j h) -> p j h", h=128)
        EI("gpsimd", lambda g: g.dma_start(vout, vin), "dma_c", 16, mark="kvout")
        W("gpsimd", "dma_c", cnt["dma_c"])
        EI("gpsimd", lambda g: g.collective_compute(
            "AllGather", mybir.AluOpType.bypass, replica_groups=[list(range(NCORE))],
            ins=[kown_d.ap().opt()], outs=[kgath_d.ap().opt()]), "coll", 1)
        EI("gpsimd", lambda g: g.collective_compute(
            "AllGather", mybir.AluOpType.bypass, replica_groups=[list(range(NCORE))],
            ins=[vown_d.ap().opt()], outs=[vgath_d.ap().opt()]), "coll", 1)
        W("sync", "coll", 2)
        for t_ in range(NT):
            r_, j_ = t_ % 8, t_ // 8
            EI("sync", lambda e, t_=t_, r_=r_, j_=j_: e.dma_start(
                K_fm[:, t_ * 128:(t_ + 1) * 128],
                kgath_d[r_ * 128:(r_ + 1) * 128, j_ * 128:(j_ + 1) * 128]), "kvld", 16)
            EI("sync", lambda e, t_=t_, r_=r_, j_=j_: e.dma_start(
                V_tm[:, t_ * 128:(t_ + 1) * 128],
                vgath_d[r_ * 1024 + j_ * 128: r_ * 1024 + (j_ + 1) * 128, 0:128]),
               "kvld", 16)
        kvld_total = cnt["kvld"]
        DUMP("dbg_K", K_fm[:])
        DUMP("dbg_V", V_tm[:])
        DUMP("dbg_xn", xn_own[:])
        DUMP("dbg_rstd", rstd_all[:])
        DUMP("dbg_nmr", nmr_all[:])
        DUMP("dbg_sx", sx2[:])
        DUMP("dbg_sxx", sxx2[:])
        DUMP("dbg_mt", mt2[:])
        DUMP("dbg_vt", vt2[:])
        DUMP("dbg_lt", lt2[:])
        DUMP("dbg_bsb", b_sb)
        DUMP("dbg_rows", rows_sb)

        WM("tensor", ("tbcp", NJ - 1))
        WM("tensor", ("bcopyA", NJ - 1))
        for j in range(NJ):
            par = j % 2
            lofs = 512 * par
            WM("tensor", ("lgev", j - 2))
            for ic in range(32):
                EI("tensor", lambda t, ic=ic, j=j, lofs=lofs: t.matmul(
                    PS[3][:, lofs:lofs + NE],
                    xn_own[:, ic * 1024 + j * 128: ic * 1024 + (j + 1) * 128],
                    rwt_bf[:, ic * NE:(ic + 1) * NE],
                    start=(ic == 0), stop=(ic == 31)),
                   **({"sem": "pe", "mark": ("lg", j)} if ic == 31 else {}))
            WM("vector", ("lg", j))
            EI("vector", lambda v, j=j, lofs=lofs: v.scalar_tensor_tensor(
                logits_all[:, j * NE:(j + 1) * NE], PS[3][:, lofs:lofs + NE],
                1.0, rbb[:], ALU.mult, ALU.add),
               "vec", 1, mark=("lgev", j))
        WM("vector", ("lgev", NJ - 1))
        l3 = logits_all.rearrange("p (j e) -> p j e", e=NE)
        EIG(lambda v: v.reduce_max(wsum[:], l3, axis=AX.X))
        mxb = wsum[:][:, :, None].broadcast_to([128, NJ, NE])
        EIG(lambda v: v.tensor_tensor(l3, l3, mxb, ALU.subtract), mark="lgsub")
        WM("scalar", "lgsub")
        WM("scalar", "kvout")
        EI("scalar", lambda s: s.activation(s_all, logits_all, AF.Exp),
           "act", 1, mark="fexpw")

        WM("tensor", "kvAe")
        WARM(40)
        for oc in range(32):
            par = oc % 2
            WM("gpsimd", ("qmm", oc - 2))
            if oc < 2:
                WM("gpsimd", ("lg", NJ - 1))
                WM("gpsimd", "kvA")
            EI("gpsimd", lambda g, oc=oc, par=par: g.dma_start(wq_slab[par], bigwq_h[oc, :, :]),
               f"dg{par}", 16, mark=("wqs", oc))
            sv, vv = marks[("wqs", oc)]
            W("tensor", sv, vv)
            if oc < 2:
                WM("tensor", "kvAe")
            WM("tensor", ("qevac", oc - 2))
            for ic in range(32):
                for half in range(2):
                    EI("tensor", lambda t, half=half, ic=ic, par=par: t.matmul(
                        PS[par][:, half * 512:half * 512 + 512],
                        wq_slab[par][:, ic * 128:(ic + 1) * 128],
                        xn_own[:, ic * 1024 + half * 512: ic * 1024 + half * 512 + 512],
                        start=(ic == 0), stop=(ic == 31)),
                       **({"sem": "pe", "mark": ("qmm", oc)} if (half == 1 and ic == 31) else {}))
            WM("scalar", ("qmm", oc))
            EI("scalar", lambda s, oc=oc, par=par: s.add(
                qpool[:, oc * 1024:(oc + 1) * 1024],
                PS[par][:], wqb[:, oc:oc + 1]),
               "act", 1, mark=("qevac", oc))
        s3 = s_all.rearrange("p (j e) -> p j e", e=NE)
        w3 = swork_w.rearrange("p (j e) -> p j e", e=NE)
        k3 = keep_w.rearrange("p (j e) -> p j e", e=NE)
        mskw = stream_f[:, 6144:6656]
        m3k = mskw.rearrange("p (j e) -> p j e", e=NE)
        wdense_w = logits_all
        WM("vector", "fexpw")
        EIG(lambda v: v.reduce_sum(ssum8[:], s3, axis=AX.X))
        EIG(lambda v: v.reciprocal(rsr8[:], ssum8[:]))
        rb3 = rsr8[:][:, :, None].broadcast_to([128, NJ, NE])
        EIG(lambda v: v.tensor_tensor(s3, s3, rb3, ALU.mult))
        EIG(lambda v: v.tensor_copy(swork_w, s_all))
        EIG(lambda v: v.memset(keep_w, 0.0))
        EIG(lambda v: v.memset(wsum[:], 0.0))
        for it in range(KTOP):
            EIG(lambda v: v.reduce_max(m8w[:], w3, axis=AX.X))
            EIG(lambda v: v.scalar_tensor_tensor(wsum[:], m8w[:], 0.0, wsum[:],
                                                 ALU.max, ALU.add))
            m3 = m8w[:][:, :, None].broadcast_to([128, NJ, NE])
            EIG(lambda v, m3=m3: v.tensor_tensor(m3k, w3, m3, ALU.is_ge))
            EIG(lambda v: v.tensor_tensor(k3, k3, m3k, ALU.add))
            EIG(lambda v: v.scalar_tensor_tensor(swork_w, mskw, -2.0, swork_w,
                                                 ALU.mult, ALU.add))
        EIG(lambda v: v.tensor_scalar(keep_w, keep_w, 1.0, None, ALU.is_ge))
        EIG(lambda v: v.tensor_tensor(wdense_w, s_all, keep_w, ALU.mult), mark="wdw")
        for j in range(NJ):
            tofs = 512 * (j % 2)
            WM("tensor", "wdw")
            WM("tensor", ("wdev", j - 2))
            EI("tensor", lambda t, j=j, tofs=tofs: t.transpose(
                PS[2][0:64, tofs:tofs + 128],
                wdense_w[:, j * 64:(j + 1) * 64], identf[:]),
               "pe", 1, mark=("wdT", j))
            WM("vector", ("wdT", j))
            EI("vector", lambda v, j=j, tofs=tofs: v.tensor_copy(
                wdT65[0:64, j * 128:(j + 1) * 128],
                PS[2][0:64, tofs:tofs + 128]),
               "vec", 1, mark=("wdev", j))
        EI("vector", lambda v: v.tensor_tensor(s1a[:], rstd_all[:], wsum[:], ALU.mult))
        EI("vector", lambda v: v.tensor_scalar(s1a[:], s1a[:], 1.0, None, ALU.add))
        EI("vector", lambda v: v.tensor_tensor(s2a[:], nmr_all[:], wsum[:], ALU.mult),
           "vec", 1, mark="epi_s")
        DUMP("dbg_wdt", wdT65[:])
        DUMP("dbg_wsum", wsum[:])
        DUMP("dbg_s1", s1a[:])
        DUMP("dbg_s2", s2a[:])
        WM("tensor", ("qevac", 31))
        WM("tensor", ("wdev", NJ - 1))
        WM("tensor", ("lgev", NJ - 1))
        WM("tensor", "epi_s")
        WM("scalar", "epi_s")
        WM("gpsimd", "epi_s")
        DUMP("dbg_q", qpool[:])

        def _bank_segs(a0, a1):
            segs = []
            for lo in range(0, 1024, 512):
                s0, s1 = max(a0, lo), min(a1, lo + 512)
                if s0 < s1:
                    segs.append((s0, s1))
            return segs

        def _prev_same_parity(b):
            return NB - 1 if (NB - 1) % 2 == b % 2 else NB - 2

        def S(h, b):
            chunks = batches[b]
            psb = PS[b % 2]
            key = ("exp", h, b - 2) if b >= 2 else ("exp", h - 1, _prev_same_parity(b))
            WM("tensor", key)
            for ci, (c, off, s) in enumerate(chunks):
                j0 = c // 8
                qbase = h * 1024 + j0 * 128
                segs = _bank_segs(off, off + s)
                for si, (s0, s1) in enumerate(segs):
                    last = (ci == len(chunks) - 1) and (si == len(segs) - 1)
                    EI("tensor", lambda t, c=c, off=off, s0=s0, s1=s1, qbase=qbase, psb=psb:
                       t.matmul(psb[:, s0:s1],
                                K_fm[:, c * 128:(c + 1) * 128],
                                qpool[:, qbase + s0 - off:qbase + s1 - off],
                                start=True, stop=True),
                       **({"sem": "pe", "mark": ("smm", h, b)} if last else {}))

        NEB = 7

        def _prev_same_modE(b):
            return NB - 1 - ((NB - 1 - b) % NEB)

        def EXP(h, b):
            chunks = batches[b]
            cols = chunks[-1][1] + chunks[-1][2]
            psb = PS[b % 2]
            eb = ebuf[b % NEB]
            key = ("vd", h, b - NEB) if b >= NEB else ("vd", h - 1, _prev_same_modE(b))
            WM("scalar", key)
            WM("scalar", ("smm", h, b))
            EI("scalar", lambda s, cols=cols, psb=psb, eb=eb:
               s.activation(eb[:, 0:cols], psb[:, 0:cols], AF.Exp, bias=neg12[:], scale=SCL),
               "act", 1, mark=("exp", h, b))

        def MASK(h, b):
            chunks = batches[b]
            eb = ebuf[b % NEB]
            WM("gpsimd", ("exp", h, b))
            n = len(chunks)
            stride = chunks[0][2]
            r0 = chunks[0][0] % 8
            assert all(s == stride for _, _, s in chunks)
            assert all((c % 8) == r0 + i for i, (c, _, _) in enumerate(chunks))
            e3 = eb[:, 0:stride * n].rearrange("p (ci m) -> p ci m", m=stride)[:, :, 0:128]
            m3 = mask8[:, r0 * 128:(r0 + n) * 128].rearrange("p (ci m) -> p ci m", m=128)
            EI("gpsimd", lambda g, e3=e3, m3=m3: g.tensor_tensor(e3, e3, m3, ALU.mult),
               "pool", 1, mark=("msk", h, b))

        def VD(h, b):
            chunks = batches[b]
            eb = ebuf[b % NEB]
            WM("tensor", ("msk", h, b))
            if b == 0:
                WM("tensor", ("oevac", h - 1))
            first = b == 0
            for ci, (c, off, s) in enumerate(chunks):
                j0 = c // 8
                base = j0 * 128
                segs = _bank_segs(base, 1024)
                for si, (s0, s1) in enumerate(segs):
                    st_ = first and ci == 0
                    e0 = off + s0 - base
                    e1 = off + s1 - base
                    EI("tensor", lambda t, c=c, s0=s0, s1=s1, e0=e0, e1=e1, st_=st_, eb=eb:
                       t.matmul(PS[2][:, s0:s1],
                                V_tm[:, c * 128:(c + 1) * 128],
                                eb[:, e0:e1],
                                start=st_, stop=False))
            nc_ = len(chunks)
            s = chunks[0][2]
            base = (chunks[0][0] // 8) * 128
            st_ = b == 0
            if s > 512:
                assert nc_ == 1
                segs = _bank_segs(base, base + s) if base + s <= 1024 else None
                segs = _bank_segs(base, 1024)
                for si, (s0, s1) in enumerate(segs):
                    EI("tensor", lambda t, s0=s0, s1=s1, base=base, st_=st_, eb=eb:
                       t.matmul(PS[3][0:1, s0:s1], onescb[:],
                                eb[:, s0 - base:s1 - base], start=st_, stop=False),
                       **({"sem": "pe", "mark": ("vd", h, b)} if si == len(segs) - 1 else {}))
            else:
                step = max(1, 512 // s)
                for g0 in range(0, nc_, step):
                    g1 = min(g0 + step, nc_)
                    last = g1 == nc_
                    if g1 - g0 == 1:
                        oap = PS[3][0:1, base:base + s]
                    else:
                        oap = PS[3][0:1, base:base + s][:, None, :].broadcast_to(
                            [1, g1 - g0, s])
                    EI("tensor", lambda t, oap=oap, g0=g0, g1=g1, s=s, st_=st_, eb=eb:
                       t.matmul(oap, onescb[:], eb[:, g0 * s:g1 * s], start=st_, stop=False),
                       **({"sem": "pe", "mark": ("vd", h, b)} if last else {}))

        def RECIP(h):
            WM("scalar", ("vd", h, NB - 1))
            WM("scalar", ("bcmm", h - 1))
            EI("scalar", lambda s: s.activation(lnrow, PS[3][0:1, 0:1024], AF.Ln), "act", 1)
            W("scalar", "act", cnt["act"])
            EI("scalar", lambda s: s.activation(rec, lnrow, AF.Exp, scale=-1.0),
               "act", 1, mark=("recip", h))

        def ENDGAME(h):
            WM("tensor", ("recip", h))
            EI("tensor", lambda t: t.matmul(PS[3][:, 0:512], onesrf[:], rec[0:1, 0:512],
                                            start=True, stop=True))
            EI("tensor", lambda t: t.matmul(PS[3][:, 512:1024], onesrf[:], rec[0:1, 512:1024],
                                            start=True, stop=True),
               "pe", 1, mark=("bcmm", h))
            WM("vector", ("bcmm", h))
            EI("vector", lambda v: v.tensor_copy(bsb, PS[3][:]))
            EI("vector", lambda v, h=h: v.tensor_tensor(qpool[:, h * 1024:(h + 1) * 1024],
                                                        PS[2][:], bsb, ALU.mult),
               "vec", 1, mark=("oevac", h))

        W("tensor", "kvld", kvld_total)
        for h in range(H):
            S(h, 0)
            S(h, 1)
            if h > 0:
                ENDGAME(h - 1)
            EXP(h, 0)
            MASK(h, 0)
            for b in range(2, NB):
                S(h, b)
                if b >= NEB:
                    VD(h, b - NEB)
                EXP(h, b - 1)
                MASK(h, b - 1)
            EXP(h, NB - 1)
            MASK(h, NB - 1)
            for b in range(NB - NEB, NB):
                VD(h, b)
            RECIP(h)
        ENDGAME(H - 1)
        barrier()
        DUMP("dbg_o", qpool[:])

        EI("gpsimd", lambda g: g.dma_start(eb65, eb65_h[:]), "dma_c", 16, mark="eb65ld")
        sv, vv = marks["eb65ld"]
        W("tensor", sv, vv)
        nstores = 0
        for p in range(2):
            for j in range(NJ):
                ei = p * NJ + j
                par = ei % 2
                WM("sync", ("t1", ei - 2))
                EI("sync", lambda e, p=p, j=j, par=par: e.dma_start(
                    xbufE[par], xtm_own_h[j, :, p * 2048:p * 2048 + 2048]),
                   f"ds{par}", 16, mark=("xE", ei))
                sv, vv = marks[("xE", ei)]
                W("vector", sv, vv)
                if p == 1 and j == 0:
                    WM("vector", ("store", 7))
                EI("vector", lambda v, j=j, par=par: v.tensor_scalar(
                    stg[j], xbufE[par], s1a[:, j:j + 1], s2a[:, j:j + 1], ALU.mult, ALU.add),
                   "vec", 1, mark=("t1", ei))
                if debug and p == 1 and j == 0:
                    sv_, vv_ = marks[("t1", ei)]
                    W("gpsimd", sv_, vv_)
                    DUMP("dbg_e1", stg[0])
                    DUMP("dbg_s1b", s1a[:])
                    W("vector", "st", cnt["st"])
            for dg in range(8):
                dmg = p * 8 + dg
                par = dmg % 2
                pspair = (PS[0], PS[1]) if par == 0 else (PS[2], PS[3])
                WM("gpsimd", ("pp", dmg - 2))
                EI("gpsimd", lambda g, dmg=dmg, par=par: g.dma_start(
                    wo_slab[par], bigwo_h[dmg, :, :]),
                   f"dg{par}", 16, mark=("wos", dmg))
                sv, vv = marks[("wos", dmg)]
                W("tensor", sv, vv)
                WM("tensor", ("ppev", dmg - 2))
                for j in range(NJ):
                    psj = pspair[0] if j < 4 else pspair[1]
                    jo = (j % 4) * 256
                    for hh in range(H):
                        EI("tensor", lambda t, j=j, hh=hh, psj=psj, jo=jo, par=par: t.matmul(
                            psj[:, jo:jo + 256],
                            qpool[:, hh * 1024 + j * 128: hh * 1024 + (j + 1) * 128],
                            wo_slab[par][:, hh * 256:(hh + 1) * 256],
                            start=(hh == 0), stop=False))
                    EI("tensor", lambda t, j=j, psj=psj, jo=jo, dmg=dmg: t.matmul(
                        psj[:, jo:jo + 256],
                        wdT65[:, j * 128:(j + 1) * 128],
                        eb65[:, dmg * 256:(dmg + 1) * 256],
                        start=False, stop=True),
                       **({"sem": "pe", "mark": ("pp", dmg)} if j == NJ - 1 else {}))
                WM("vector", ("pp", dmg))
                for j in range(NJ):
                    psj = pspair[0] if j < 4 else pspair[1]
                    jo = (j % 4) * 256
                    kw = {"sem": "vec", "mark": ("ppev", dmg)} if j == NJ - 1 else                          ({"sem": "vec", "mark": ("ppevj", dmg, j)} if dg == 7 else {})
                    EI("vector", lambda v, j=j, dg=dg, psj=psj, jo=jo: v.tensor_tensor(
                        stg[j][:, dg * 256:(dg + 1) * 256],
                        stg[j][:, dg * 256:(dg + 1) * 256],
                        psj[:, jo:jo + 256], ALU.add), **kw)
            if debug and p == 1:
                sv, vv = marks[("ppev", p * 8 + 7)]
                W("gpsimd", sv, vv)
                DUMP("dbg_e2", stg[0])
                W("sync", "st", cnt["st"])
            for j in range(NJ):
                key = ("ppevj", p * 8 + 7, j) if j < NJ - 1 else ("ppev", p * 8 + 7)
                sv, vv = marks[key]
                W("sync", sv, vv)
                EI("sync", lambda e, p=p, j=j: e.dma_start(
                    out_h[j, :, p * 2048:p * 2048 + 2048], stg[j]),
                   "st", 16, mark=("store", nstores))
                nstores += 1
        W("sync", "st", cnt["st"])

        ninstr = sum(len(v) for v in q.values())
        print(f"[kernel build] instructions: {ninstr} "
              f"({ {k: len(v) for k, v in q.items()} })")

        block = stack.enter_context(nc.Block())

        @block.sync
        def _(e):
            for f in q["sync"]:
                f(e)

        @block.gpsimd
        def _(e):
            for f in q["gpsimd"]:
                f(e)

        @block.tensor
        def _(e):
            for f in q["tensor"]:
                f(e)

        @block.scalar
        def _(e):
            for f in q["scalar"]:
                f(e)

        @block.vector
        def _(e):
            for f in q["vector"]:
                f(e)

    return nc


def _prep_inputs(L0_attn_x, L0_attn_wq_w, L0_attn_wq_b, L0_attn_wk_w, L0_attn_wk_b,
                 L0_attn_wv_w, L0_attn_wv_b, L0_attn_wo_w, L0_attn_wo_b,
                 L0_ffn_router_weight, L0_ffn_router_bias, L0_ffn_expert_bias):
    f32 = np.float32
    x = np.ascontiguousarray(np.asarray(L0_attn_x, dtype=f32).reshape(T, D))
    wq = np.asarray(L0_attn_wq_w, dtype=f32)
    wk = np.asarray(L0_attn_wk_w, dtype=f32)
    wv = np.asarray(L0_attn_wv_w, dtype=f32)
    wo = np.asarray(L0_attn_wo_w, dtype=f32)
    rw = np.asarray(L0_ffn_router_weight, dtype=f32)
    eb = np.asarray(L0_ffn_expert_bias, dtype=f32)
    wqb_ = np.asarray(L0_attn_wq_b, dtype=f32)
    wkb_ = np.asarray(L0_attn_wk_b, dtype=f32)
    wvb_ = np.asarray(L0_attn_wv_b, dtype=f32)
    wob_ = np.asarray(L0_attn_wo_b, dtype=f32)
    rbb_ = np.asarray(L0_ffn_router_bias, dtype=f32)

    xT = np.ascontiguousarray(x.T)
    xfm_all = np.ascontiguousarray(
        xT.reshape(32, 128, NT, 128).transpose(2, 1, 0, 3).reshape(NT, 128, D))
    xtm_all = np.ascontiguousarray(x.reshape(NT, 128, D))
    wqT = np.ascontiguousarray(wq.T)
    bigwq = np.ascontiguousarray(
        wqT.reshape(32, 128, 32, 128).transpose(2, 1, 0, 3).reshape(32, 128, D))
    woT = np.ascontiguousarray(wo.T)
    bigwo = np.ascontiguousarray(
        woT.reshape(32, 128, 16, 256).transpose(2, 1, 0, 3).reshape(16, 128, 8192))
    wkt = np.ascontiguousarray(wk.T.reshape(32, 128, 128).transpose(1, 0, 2).reshape(128, D))
    wvt = np.ascontiguousarray(wv.T.reshape(32, 128, 128).transpose(1, 0, 2).reshape(128, D))
    rwt = np.ascontiguousarray(rw.T.reshape(32, 128, NE).transpose(1, 0, 2).reshape(128, 2048))
    eb65 = np.ascontiguousarray(np.concatenate([eb, wob_[None, :]], axis=0))
    wqb_in = np.ascontiguousarray(wqb_.reshape(32, 128).T)
    wkb_in = np.ascontiguousarray(wkb_.reshape(128, 1))
    wvb_in = np.ascontiguousarray(wvb_.reshape(1, 128))
    rbb_in = np.ascontiguousarray(np.broadcast_to(rbb_[None, :], (128, NE)))
    identf = np.eye(128, dtype=f32)

    tril = (np.arange(128)[:, None] <= np.arange(128)[None, :]).astype(f32)
    onesb = np.ones((128, 128), dtype=f32)
    zerosb = np.zeros((128, 128), dtype=f32)

    shared = dict(bigwq=bigwq, bigwo=bigwo,
                  wkt=wkt, wvt=wvt, rwt=rwt, eb65=eb65, wqb=wqb_in, wkb=wkb_in,
                  wvb=wvb_in, rbb=rbb_in, identf=identf)
    in_maps = []
    for i in range(NCORE):
        own = [8 * j + i for j in range(NJ)]
        m8 = np.concatenate(
            [onesb if r < i else (tril if r == i else zerosb) for r in range(8)],
            axis=1).astype(ml_dtypes.bfloat16)
        d = dict(shared)
        d["xfm_own"] = np.ascontiguousarray(xfm_all[own])
        d["xtm_own"] = np.ascontiguousarray(xtm_all[own])
        d["mask8"] = m8
        in_maps.append(d)
    return in_maps


def kernel(**inputs):
    in_maps = _prep_inputs(**inputs)
    nc = build_nc()
    res = run_bass_kernel_spmd(nc, in_maps, core_ids=list(range(NCORE)))
    full = np.empty((T, D), dtype=np.float32)
    for i, r in enumerate(res.results):
        o = r["out"]
        for j in range(NJ):
            t = 8 * j + i
            full[t * 128:(t + 1) * 128] = o[j]
    return full.reshape(1, T, D)


# version 20
# speedup vs baseline: 1.0021x; 1.0001x over previous
import math
from collections import defaultdict
from contextlib import ExitStack

import numpy as np
import ml_dtypes

import concourse.bass as bass
import concourse.mybir as mybir
from concourse.bass_utils import run_bass_kernel_spmd

F32 = mybir.dt.float32
BF16 = mybir.dt.bfloat16
AF = mybir.ActivationFunctionType
ALU = mybir.AluOpType
AX = mybir.AxisListType

D = 4096
T = 8192
H = 32
HD = 128
NT = 64
NJ = 8
NCORE = 8
NE = 64
KTOP = 8
SCL = 1.0 / math.sqrt(128.0)
EPS = 1e-5
NEGC = -12.0


def _batches():
    sizes = [(8 - c // 8) * 128 for c in range(64)]
    batches, cur, cols = [], [], 0
    for c, s in enumerate(sizes):
        if cols + s > 1024 or (cur and s != cur[0][2]):
            batches.append(cur)
            cur, cols = [], 0
        cur.append((c, cols, s))
        cols += s
    batches.append(cur)
    return batches


def build_nc(debug=False):
    nc = bass.Bass("TRN2", target_bir_lowering=False, debug=False)
    dr = lambda n, s, dt=F32, kind="ExternalInput": nc.dram_tensor(n, s, dt, kind=kind)

    xfm_own_h = dr("xfm_own", [NJ, 128, D])
    xtm_own_h = dr("xtm_own", [NJ, 128, D])
    bigwq_h = dr("bigwq", [32, 128, D])
    bigwo_h = dr("bigwo", [16, 128, 8192])
    wkt_h = dr("wkt", [128, D])
    wvt_h = dr("wvt", [128, D])
    rwt_h = dr("rwt", [128, 2048])
    eb65_h = dr("eb65", [65, D])
    wqb_h = dr("wqb", [128, 32])
    wkb_h = dr("wkb", [128, 1])
    wvb_h = dr("wvb", [1, 128])
    rbb_h = dr("rbb", [128, NE])
    identf_h = dr("identf", [128, 128])
    mask8_h = dr("mask8", [128, 1024], BF16)
    out_h = dr("out", [NJ, 128, D], F32, kind="ExternalOutput")
    kown_d = nc.dram_tensor("kown_b", [128, 1024], BF16)
    kgath_d = nc.dram_tensor("kgath_b", [1024, 1024], BF16, addr_space="Shared")
    vown_d = nc.dram_tensor("vown_b", [1024, 128], BF16)
    vgath_d = nc.dram_tensor("vgath_b", [T, 128], BF16, addr_space="Shared")
    dbg_h = {}
    if debug:
        for n, s, dt in [("dbg_rstd", [128, NJ], F32), ("dbg_nmr", [128, NJ], F32),
                         ("dbg_xn", [128, 32768], BF16), ("dbg_K", [128, T], BF16),
                         ("dbg_V", [128, T], BF16), ("dbg_q", [128, 32768], BF16),
                         ("dbg_o", [128, 32768], BF16), ("dbg_wdt", [65, 1024], BF16),
                         ("dbg_wsum", [128, NJ], F32), ("dbg_s1", [128, NJ], F32),
                         ("dbg_s2", [128, NJ], F32), ("dbg_sx", [128, 2], F32),
                         ("dbg_sxx", [128, 2], F32), ("dbg_mt", [128, 2], F32),
                         ("dbg_vt", [128, 2], F32), ("dbg_lt", [128, 2], F32),
                         ("dbg_bsb", [128, 512], F32), ("dbg_rows", [1, 512], F32),
                         ("dbg_lg", [128, 512], F32), ("dbg_snap", [128, 512], F32),
                         ("dbg_nm8", [128, NJ], F32), ("dbg_ss8", [128, NJ], F32),
                         ("dbg_rs8", [128, NJ], F32), ("dbg_e1", [128, 2048], F32),
                         ("dbg_e2", [128, 2048], F32), ("dbg_e3", [128, 256], F32),
                         ("dbg_s1b", [128, NJ], F32)]:
            dbg_h[n] = dr(n, s, dt, kind="ExternalOutput")

    with ExitStack() as stack:
        sb = lambda n, s, dt: stack.enter_context(nc.sbuf_tensor(n, s, dt))
        psa = lambda n: stack.enter_context(nc.psum_tensor(n, [128, 1024], F32))
        mksem = lambda n: stack.enter_context(nc.semaphore(n))

        identf = sb("identf_s", [128, 128], F32)
        mask8 = sb("mask8_s", [128, 1024], BF16)
        wqb = sb("wqb_s", [128, 32], F32)
        wkb = sb("wkb_s", [128, 1], F32)
        wvb = sb("wvb_s", [1, 128], BF16)
        rbb = sb("rbb_s", [128, NE], F32)
        wdT65 = sb("wdT65_s", [65, 1024], BF16)
        neg12 = sb("neg12_s", [128, 1], F32)
        onescb = sb("onescb_s", [128, 1], BF16)
        onesrb = sb("onesrb_s", [1, 128], BF16)
        onesrf = sb("onesrf_s", [1, 128], F32)
        rstd_all = sb("rstd_all_s", [128, NJ], F32)
        nmr_all = sb("nmr_all_s", [128, NJ], F32)
        wsum = sb("wsum_s", [128, NJ], F32)
        s1a = sb("s1a_s", [128, NJ], F32)
        s2a = sb("s2a_s", [128, NJ], F32)
        sx2 = sb("sx2_s", [128, 2], F32)
        sxx2 = sb("sxx2_s", [128, 2], F32)
        mt2 = sb("mt2_s", [128, 2], F32)
        mq2 = sb("mq2_s", [128, 2], F32)
        vt2 = sb("vt2_s", [128, 2], F32)
        lt2 = sb("lt2_s", [128, 2], F32)
        rst2 = sb("rst2_s", [128, 2], F32)
        nmr2 = sb("nmr2_s", [128, 2], F32)
        logits_sb = sb("logits_s", [128, NE], F32)
        if debug:
            dbg_lg_sb = sb("dbg_lg_sb", [128, 512], F32)
            dbg_snap_sb = sb("dbg_snap_sb", [128, 512], F32)
            dbg_nm8_sb = sb("dbg_nm8_sb", [128, NJ], F32)
            dbg_ss8_sb = sb("dbg_ss8_sb", [128, NJ], F32)
            dbg_rs8_sb = sb("dbg_rs8_sb", [128, NJ], F32)
        s_sb = sb("ssm_s", [128, NE], F32)
        swork = sb("swork_s", [128, NE], F32)
        keep = sb("keep_s", [128, NE], F32)
        mske = sb("mske_s", [128, NE], F32)
        wdense = sb("wdense_s", [128, NE], F32)
        m8 = sb("m8_s", [128, 1], F32)
        ssum8 = sb("ssum8_s", [128, NJ], F32)
        rsr8 = sb("rsr8_s", [128, NJ], F32)
        m8w = sb("m8w_s", [128, NJ], F32)
        nmax = sb("nmax_s", [128, 1], F32)
        ssum = sb("ssum_s", [128, 1], F32)
        rsr = sb("rsr_s", [128, 1], F32)

        K_fm = sb("K_fm", [128, T], BF16)
        V_tm = sb("V_tm", [128, T], BF16)
        qpool = sb("qpool", [128, 32768], BF16)
        xn_own = sb("xn_own", [128, 32768], BF16)
        stream = sb("stream", [128, 16384], BF16)

        qpool_f = qpool.bitcast(F32)
        xn_f = xn_own.bitcast(F32)
        stream_f = stream.bitcast(F32)
        kv_f = K_fm.bitcast(F32)

        assert list(qpool_f.shape) == [128, 16384], qpool_f.shape

        xfm_buf = [qpool[:, k * 4096:(k + 1) * 4096] for k in range(3)]
        xsq_buf = [qpool[:, 12288 + k * 4096:12288 + (k + 1) * 4096] for k in range(3)]
        rwt_bf = stream[:, 0:2048]
        wq_slab = [stream[:, 0:4096], stream[:, 4096:8192]]
        ebuf = [stream[:, 8192:9216], stream[:, 9216:10240], stream[:, 0:1024],
                stream[:, 1024:2048], stream[:, 2048:3072], stream[:, 3072:4096],
                stream[:, 4096:5120]]
        kown_sb = stream[:, 8192:9216]
        vown_sb = stream[:, 9216:10240]
        bsb = stream_f[:, 5120:6144]
        rec = stream_f[0:1, 6144:7168]
        lnrow = stream_f[0:1, 7168:8192]
        wo_slab = [stream[:, 0:8192], stream[:, 8192:16384]]
        stg = [xn_f[:, j * 2048:(j + 1) * 2048] for j in range(NJ)]
        xbufE = [kv_f[:, 0:2048], kv_f[:, 2048:4096]]
        eb65 = V_tm[0:65, 0:4096]
        b_sb = stream_f[:, 5120:5632]
        rows_sb = stream_f[0:1, 5632:6144]
        logits_all = stream_f[:, 5632:6144]
        s_all = stream_f[:, 4096:4608]
        swork_w = stream_f[:, 4608:5120]
        keep_w = stream_f[:, 5120:5632]

        PS = [psa(f"PS{i}") for i in range(4)]

        sems = {n: mksem(n) for n in ["ds0", "ds1", "dg0", "dg1", "dma_c", "st",
                                      "pe", "act", "vec", "pool", "coll", "kvld"]}
        cnt = defaultdict(int)
        marks = {}
        q = {"sync": [], "gpsimd": [], "tensor": [], "scalar": [], "vector": []}

        def EI(eng, bf, sem=None, k=1, mark=None):
            if eng == "vector" and sem is None:
                sem = "vec"
            if sem is not None:
                cnt[sem] += k
                if mark is not None:
                    marks[mark] = (sem, cnt[sem])
                sh = sems[sem]

                def g(e, bf=bf, sh=sh, k=k):
                    bf(e).then_inc(sh, k)
            else:
                def g(e, bf=bf):
                    bf(e)
            q[eng].append(g)
            if eng == "vector":
                W("vector", "vec", cnt["vec"])

        def W(eng, sem, v):
            if v > 0:
                q[eng].append(lambda e, sh=sems[sem], v=v: e.wait_ge(sh, v))

        def WM(eng, key):
            if key in marks:
                s, v = marks[key]
                W(eng, s, v)

        def EIG(bf, mark=None):
            EI("vector", bf, "vec", 1, mark=mark)

        def barrier():
            snap = [(s, cnt[s]) for s in sems if s not in ("coll", "kvld")]
            for eng in q:
                for s, v in snap:
                    W(eng, s, v)

        def DUMP(name, ap):
            if debug:
                EI("gpsimd", lambda g, ap=ap, h=dbg_h[name]: g.dma_start(h[:], ap), "st", 16)

        EI("sync", lambda e: e.dma_start(identf[:], identf_h[:]), "ds0", 16)
        EI("sync", lambda e: e.dma_start(mask8[:], mask8_h[:]), "ds0", 16)
        EI("sync", lambda e: e.dma_start(wqb[:], wqb_h[:]), "ds0", 16)
        EI("sync", lambda e: e.dma_start(wkb[:], wkb_h[:]), "ds0", 16)
        EI("gpsimd", lambda g: g.dma_start(wvb[:], wvb_h[:]), "dma_c", 16)
        EI("sync", lambda e: e.dma_start(rbb[:], rbb_h[:]), "ds0", 16)
        EI("gpsimd", lambda g: g.dma_start(rwt_bf, rwt_h[:]), "dma_c", 16)
        kt_bf = stream[:, 2048:6144]
        vt_bf = stream[:, 6144:10240]
        EI("gpsimd", lambda g: g.dma_start(kt_bf, wkt_h[:]), "dma_c", 16)
        EI("gpsimd", lambda g: g.dma_start(vt_bf, wvt_h[:]), "dma_c", 16)
        EI("gpsimd", lambda g: g.memset(neg12[:], NEGC))
        EI("gpsimd", lambda g: g.memset(onescb[:], 1.0))
        EI("gpsimd", lambda g: g.memset(onesrb[:], 1.0))
        EI("gpsimd", lambda g: g.memset(onesrf[:], 1.0))
        EI("gpsimd", lambda g: g.memset(wdT65[64:65, :], 1.0), "dma_c", 1, mark="consts")
        cdone = cnt["dma_c"]
        sdone = cnt["ds0"]
        for eng in q:
            W(eng, "dma_c", cdone)
            W(eng, "ds0", sdone)

        batches = _batches()
        NB = len(batches)

        def WARM(n):
            for _ in range(n):
                EI("tensor", lambda t: t.matmul(PS[0][:, 0:512], mask8[:, 0:128],
                                                mask8[:, 0:512], start=True, stop=True))

        rows_b = stream_f[0:1, 6144:6656]

        def phaseA_tile(ti, own, a_or_j):
            par = ti % 2
            bpar = ti % 3
            pofs = 512 * par
            bofs = 512 * par
            idx = a_or_j
            m_row = rows_sb[0:1, 256 * par:256 * par + 128]
            v_row = rows_sb[0:1, 256 * par + 128:256 * par + 256]
            rstd_row = rows_b[0:1, 256 * par:256 * par + 128]
            nmr_row = rows_b[0:1, 256 * par + 128:256 * par + 256]

            WM("gpsimd", ("xnmul", ti - 3))
            WM("gpsimd", ("sq", ti - 3))
            EI("gpsimd", lambda g: g.dma_start(xfm_buf[bpar], xfm_own_h[idx, :, :]),
               ["dg0", "dg1", "dma_c"][bpar], 16, mark=("xfm", ti))
            gv, gvv = marks[("xfm", ti)]
            W("scalar", gv, gvv)
            WM("scalar", ("sxxm", ti - 3))
            EI("scalar", lambda s: s.activation(xsq_buf[bpar], xfm_buf[bpar], AF.Square),
               "act", 1, mark=("sq", ti))
            W("tensor", gv, gvv)
            WM("tensor", ("tbcp", ti - 2))
            oap = PS[3][0:1, pofs:pofs + 128][:, None, :].broadcast_to([1, 4, 128])
            for b in range(8):
                EI("tensor", lambda t, b=b, oap=oap: t.matmul(
                    oap, onescb[:], xfm_buf[bpar][:, b * 512:(b + 1) * 512],
                    start=(b == 0), stop=(b == 7)),
                   **({"sem": "pe", "mark": ("sxm", ti)} if b == 7 else {}))
            WM("tensor", ("sq", ti))
            oap2 = PS[3][0:1, pofs + 128:pofs + 256][:, None, :].broadcast_to([1, 4, 128])
            for b in range(8):
                EI("tensor", lambda t, b=b, oap2=oap2: t.matmul(
                    oap2, onescb[:], xsq_buf[bpar][:, b * 512:(b + 1) * 512],
                    start=(b == 0), stop=(b == 7)),
                   **({"sem": "pe", "mark": ("sxxm", ti)} if b == 7 else {}))
            WM("vector", ("sxxm", ti))
            WM("vector", ("bcmmA", ti - 2))
            EI("vector", lambda v: v.tensor_copy(rows_sb[0:1, 256 * par:256 * par + 256],
                                                 PS[3][0:1, pofs:pofs + 256]),
               "vec", 1, mark=("rowsA", ti))
            EIG(lambda v: v.tensor_scalar(m_row, m_row, 1.0 / D, None, ALU.mult))
            EIG(lambda v: v.tensor_tensor(rstd_row, m_row, m_row, ALU.mult))
            EIG(lambda v: v.tensor_scalar(v_row, v_row, 1.0 / D, None, ALU.mult))
            EIG(lambda v: v.tensor_tensor(v_row, v_row, rstd_row, ALU.subtract))
            EIG(lambda v: v.tensor_scalar(v_row, v_row, EPS, None, ALU.add),
                mark=("statA", ti))
            WM("scalar", ("statA", ti))
            EI("scalar", lambda s: s.activation(nmr_row, v_row, AF.Ln), "act", 1)
            W("scalar", "act", cnt["act"])
            EI("scalar", lambda s: s.activation(rstd_row, nmr_row, AF.Exp, scale=-0.5),
               "act", 1, mark=("rstdA", ti))
            WM("vector", ("rstdA", ti))
            EI("vector", lambda v: v.scalar_tensor_tensor(nmr_row, m_row, -1.0, rstd_row,
                                                          ALU.mult, ALU.mult),
               "vec", 1, mark=("nmrA", ti))
            WM("tensor", ("nmrA", ti))
            WM("tensor", ("bcopyA", ti - 2))
            EI("tensor", lambda t: t.matmul(PS[2][:, bofs:bofs + 256], onesrf[:],
                                            rows_b[0:1, 256 * par:256 * par + 256],
                                            start=True, stop=True),
               "pe", 1, mark=("bcmmA", ti))
            EI("tensor", lambda t: t.matmul(PS[3][:, pofs + 256:pofs + 257], rstd_row,
                                            onesrf[0:1, 0:1], start=True, stop=True))
            EI("tensor", lambda t: t.matmul(PS[3][:, pofs + 257:pofs + 258], nmr_row,
                                            onesrf[0:1, 0:1], start=True, stop=True),
               "pe", 1, mark=("tbA", ti))
            WM("vector", ("bcmmA", ti))
            WM("vector", ("xnadd", ti - 2))
            EI("vector", lambda v: v.tensor_copy(b_sb[:, 256 * par:256 * par + 256],
                                                 PS[2][:, bofs:bofs + 256]),
               "vec", 1, mark=("bcopyA", ti))
            WM("vector", ("tbA", ti))
            EI("vector", lambda v: v.tensor_copy(rstd_all[:, idx:idx + 1],
                                                 PS[3][:, pofs + 256:pofs + 257]))
            EI("vector", lambda v: v.tensor_copy(nmr_all[:, idx:idx + 1],
                                                 PS[3][:, pofs + 257:pofs + 258]),
               "vec", 1, mark=("tbcp", ti))
            x3 = (xfm_buf[bpar]).rearrange("p (c m) -> p c m", m=128)
            br = b_sb[:, 256 * par:256 * par + 128][:, None, :].broadcast_to([128, 32, 128])
            bn = b_sb[:, 256 * par + 128:256 * par + 256][:, None, :].broadcast_to([128, 32, 128])
            o3 = xn_own[:].rearrange("p (ic t) -> p ic t", t=1024)[:, :, idx * 128:(idx + 1) * 128]
            EI("vector", lambda v, o3=o3, x3=x3, br=br: v.tensor_tensor(o3, x3, br, ALU.mult),
               "vec", 1, mark=("xnmul", ti))
            EI("vector", lambda v, o3=o3, bn=bn: v.tensor_tensor(o3, o3, bn, ALU.add),
               "vec", 1, mark=("xnadd", ti))

        WARM(16)
        for j in range(NJ):
            phaseA_tile(j, True, j)
            WARM(10)
        WM("tensor", ("xnadd", NJ - 1))
        for half in range(2):
            for ic in range(32):
                EI("tensor", lambda t, half=half, ic=ic: t.matmul(
                    PS[0][:, half * 512:half * 512 + 512],
                    kt_bf[:, ic * 128:(ic + 1) * 128],
                    xn_own[:, ic * 1024 + half * 512: ic * 1024 + half * 512 + 512],
                    start=(ic == 0), stop=(ic == 31)))
        for j in range(NJ):
            for ic in range(32):
                EI("tensor", lambda t, j=j, ic=ic: t.matmul(
                    PS[1][:, j * 128:(j + 1) * 128],
                    xn_own[:, ic * 1024 + j * 128: ic * 1024 + (j + 1) * 128],
                    vt_bf[:, ic * 128:(ic + 1) * 128],
                    start=(ic == 0), stop=False))
            EI("tensor", lambda t, j=j: t.matmul(
                PS[1][:, j * 128:(j + 1) * 128], onesrb[:], wvb[:],
                start=False, stop=True),
               **({"sem": "pe", "mark": "kvA"} if j == NJ - 1 else {}))
        WM("scalar", "kvA")
        EI("scalar", lambda s: s.add(kown_sb, PS[0][:], wkb[:]), "act", 1)
        EI("scalar", lambda s: s.activation(vown_sb, PS[1][:], AF.Copy),
           "act", 1, mark="kvAe")
        WM("gpsimd", "kvAe")
        EI("gpsimd", lambda g: g.dma_start(kown_d[:], kown_sb), "dma_c", 16)
        vout = vown_d[:].rearrange("(j p) h -> p j h", p=128)
        vin = vown_sb.rearrange("p (j h) -> p j h", h=128)
        EI("gpsimd", lambda g: g.dma_start(vout, vin), "dma_c", 16, mark="kvout")
        W("gpsimd", "dma_c", cnt["dma_c"])
        EI("gpsimd", lambda g: g.collective_compute(
            "AllGather", mybir.AluOpType.bypass, replica_groups=[list(range(NCORE))],
            ins=[kown_d.ap().opt()], outs=[kgath_d.ap().opt()]), "coll", 1)
        EI("gpsimd", lambda g: g.collective_compute(
            "AllGather", mybir.AluOpType.bypass, replica_groups=[list(range(NCORE))],
            ins=[vown_d.ap().opt()], outs=[vgath_d.ap().opt()]), "coll", 1)
        W("sync", "coll", 2)
        for t_ in range(NT):
            r_, j_ = t_ % 8, t_ // 8
            EI("sync", lambda e, t_=t_, r_=r_, j_=j_: e.dma_start(
                K_fm[:, t_ * 128:(t_ + 1) * 128],
                kgath_d[r_ * 128:(r_ + 1) * 128, j_ * 128:(j_ + 1) * 128]), "kvld", 16)
            EI("sync", lambda e, t_=t_, r_=r_, j_=j_: e.dma_start(
                V_tm[:, t_ * 128:(t_ + 1) * 128],
                vgath_d[r_ * 1024 + j_ * 128: r_ * 1024 + (j_ + 1) * 128, 0:128]),
               "kvld", 16)
        kvld_total = cnt["kvld"]
        DUMP("dbg_K", K_fm[:])
        DUMP("dbg_V", V_tm[:])
        DUMP("dbg_xn", xn_own[:])
        DUMP("dbg_rstd", rstd_all[:])
        DUMP("dbg_nmr", nmr_all[:])
        DUMP("dbg_sx", sx2[:])
        DUMP("dbg_sxx", sxx2[:])
        DUMP("dbg_mt", mt2[:])
        DUMP("dbg_vt", vt2[:])
        DUMP("dbg_lt", lt2[:])
        DUMP("dbg_bsb", b_sb)
        DUMP("dbg_rows", rows_sb)

        WM("tensor", ("tbcp", NJ - 1))
        WM("tensor", ("bcopyA", NJ - 1))
        for j in range(NJ):
            par = j % 2
            lofs = 512 * par
            WM("tensor", ("lgev", j - 2))
            for ic in range(32):
                EI("tensor", lambda t, ic=ic, j=j, lofs=lofs: t.matmul(
                    PS[3][:, lofs:lofs + NE],
                    xn_own[:, ic * 1024 + j * 128: ic * 1024 + (j + 1) * 128],
                    rwt_bf[:, ic * NE:(ic + 1) * NE],
                    start=(ic == 0), stop=(ic == 31)),
                   **({"sem": "pe", "mark": ("lg", j)} if ic == 31 else {}))
            WM("vector", ("lg", j))
            EI("vector", lambda v, j=j, lofs=lofs: v.scalar_tensor_tensor(
                logits_all[:, j * NE:(j + 1) * NE], PS[3][:, lofs:lofs + NE],
                1.0, rbb[:], ALU.mult, ALU.add),
               "vec", 1, mark=("lgev", j))
        WM("vector", ("lgev", NJ - 1))
        l3 = logits_all.rearrange("p (j e) -> p j e", e=NE)
        EIG(lambda v: v.reduce_max(wsum[:], l3, axis=AX.X))
        mxb = wsum[:][:, :, None].broadcast_to([128, NJ, NE])
        EIG(lambda v: v.tensor_tensor(l3, l3, mxb, ALU.subtract), mark="lgsub")
        WM("scalar", "lgsub")
        WM("scalar", "kvout")
        EI("scalar", lambda s: s.activation(s_all, logits_all, AF.Exp),
           "act", 1, mark="fexpw")

        WM("tensor", "kvAe")
        WARM(40)
        for oc in range(32):
            par = oc % 2
            WM("gpsimd", ("qmm", oc - 2))
            if oc < 2:
                WM("gpsimd", ("lg", NJ - 1))
                WM("gpsimd", "kvA")
            EI("gpsimd", lambda g, oc=oc, par=par: g.dma_start(wq_slab[par], bigwq_h[oc, :, :]),
               f"dg{par}", 16, mark=("wqs", oc))
            sv, vv = marks[("wqs", oc)]
            W("tensor", sv, vv)
            if oc < 2:
                WM("tensor", "kvAe")
            WM("tensor", ("qevac", oc - 2))
            for ic in range(32):
                for half in range(2):
                    EI("tensor", lambda t, half=half, ic=ic, par=par: t.matmul(
                        PS[par][:, half * 512:half * 512 + 512],
                        wq_slab[par][:, ic * 128:(ic + 1) * 128],
                        xn_own[:, ic * 1024 + half * 512: ic * 1024 + half * 512 + 512],
                        start=(ic == 0), stop=(ic == 31)),
                       **({"sem": "pe", "mark": ("qmm", oc)} if (half == 1 and ic == 31) else {}))
            WM("scalar", ("qmm", oc))
            EI("scalar", lambda s, oc=oc, par=par: s.add(
                qpool[:, oc * 1024:(oc + 1) * 1024],
                PS[par][:], wqb[:, oc:oc + 1]),
               "act", 1, mark=("qevac", oc))
        s3 = s_all.rearrange("p (j e) -> p j e", e=NE)
        w3 = swork_w.rearrange("p (j e) -> p j e", e=NE)
        k3 = keep_w.rearrange("p (j e) -> p j e", e=NE)
        mskw = stream_f[:, 6144:6656]
        m3k = mskw.rearrange("p (j e) -> p j e", e=NE)
        wdense_w = logits_all
        WM("vector", "fexpw")
        EIG(lambda v: v.reduce_sum(ssum8[:], s3, axis=AX.X))
        EIG(lambda v: v.reciprocal(rsr8[:], ssum8[:]))
        rb3 = rsr8[:][:, :, None].broadcast_to([128, NJ, NE])
        EIG(lambda v: v.tensor_tensor(s3, s3, rb3, ALU.mult))
        EIG(lambda v: v.tensor_copy(swork_w, s_all))
        EIG(lambda v: v.memset(keep_w, 0.0))
        EIG(lambda v: v.memset(wsum[:], 0.0))
        for it in range(KTOP):
            EIG(lambda v: v.reduce_max(m8w[:], w3, axis=AX.X))
            EIG(lambda v: v.scalar_tensor_tensor(wsum[:], m8w[:], 0.0, wsum[:],
                                                 ALU.max, ALU.add))
            m3 = m8w[:][:, :, None].broadcast_to([128, NJ, NE])
            EIG(lambda v, m3=m3: v.tensor_tensor(m3k, w3, m3, ALU.is_ge))
            EIG(lambda v: v.tensor_tensor(k3, k3, m3k, ALU.add))
            EIG(lambda v: v.scalar_tensor_tensor(swork_w, mskw, -2.0, swork_w,
                                                 ALU.mult, ALU.add))
        EIG(lambda v: v.tensor_scalar(keep_w, keep_w, 1.0, None, ALU.is_ge))
        EIG(lambda v: v.tensor_tensor(wdense_w, s_all, keep_w, ALU.mult), mark="wdw")
        for j in range(NJ):
            tofs = 512 * (j % 2)
            WM("tensor", "wdw")
            WM("tensor", ("wdev", j - 2))
            EI("tensor", lambda t, j=j, tofs=tofs: t.transpose(
                PS[2][0:64, tofs:tofs + 128],
                wdense_w[:, j * 64:(j + 1) * 64], identf[:]),
               "pe", 1, mark=("wdT", j))
            WM("vector", ("wdT", j))
            EI("vector", lambda v, j=j, tofs=tofs: v.tensor_copy(
                wdT65[0:64, j * 128:(j + 1) * 128],
                PS[2][0:64, tofs:tofs + 128]),
               "vec", 1, mark=("wdev", j))
        EI("vector", lambda v: v.tensor_tensor(s1a[:], rstd_all[:], wsum[:], ALU.mult))
        EI("vector", lambda v: v.tensor_scalar(s1a[:], s1a[:], 1.0, None, ALU.add))
        EI("vector", lambda v: v.tensor_tensor(s2a[:], nmr_all[:], wsum[:], ALU.mult),
           "vec", 1, mark="epi_s")
        DUMP("dbg_wdt", wdT65[:])
        DUMP("dbg_wsum", wsum[:])
        DUMP("dbg_s1", s1a[:])
        DUMP("dbg_s2", s2a[:])
        WM("tensor", ("qevac", 31))
        WM("tensor", ("wdev", NJ - 1))
        WM("tensor", ("lgev", NJ - 1))
        WM("tensor", "epi_s")
        WM("scalar", "epi_s")
        WM("gpsimd", "epi_s")
        DUMP("dbg_q", qpool[:])

        def _bank_segs(a0, a1):
            segs = []
            for lo in range(0, 1024, 512):
                s0, s1 = max(a0, lo), min(a1, lo + 512)
                if s0 < s1:
                    segs.append((s0, s1))
            return segs

        def _prev_same_parity(b):
            return NB - 1 if (NB - 1) % 2 == b % 2 else NB - 2

        def S(h, b):
            chunks = batches[b]
            psb = PS[b % 2]
            key = ("exp", h, b - 2) if b >= 2 else ("exp", h - 1, _prev_same_parity(b))
            WM("tensor", key)
            for ci, (c, off, s) in enumerate(chunks):
                j0 = c // 8
                qbase = h * 1024 + j0 * 128
                segs = _bank_segs(off, off + s)
                for si, (s0, s1) in enumerate(segs):
                    last = (ci == len(chunks) - 1) and (si == len(segs) - 1)
                    EI("tensor", lambda t, c=c, off=off, s0=s0, s1=s1, qbase=qbase, psb=psb:
                       t.matmul(psb[:, s0:s1],
                                K_fm[:, c * 128:(c + 1) * 128],
                                qpool[:, qbase + s0 - off:qbase + s1 - off],
                                start=True, stop=True),
                       **({"sem": "pe", "mark": ("smm", h, b)} if last else {}))

        NEB = 7

        def _prev_same_modE(b):
            return NB - 1 - ((NB - 1 - b) % NEB)

        def EXP(h, b):
            chunks = batches[b]
            cols = chunks[-1][1] + chunks[-1][2]
            psb = PS[b % 2]
            eb = ebuf[b % NEB]
            key = ("vd", h, b - NEB) if b >= NEB else ("vd", h - 1, _prev_same_modE(b))
            WM("scalar", key)
            WM("scalar", ("smm", h, b))
            EI("scalar", lambda s, cols=cols, psb=psb, eb=eb:
               s.activation(eb[:, 0:cols], psb[:, 0:cols], AF.Exp, bias=neg12[:], scale=SCL),
               "act", 1, mark=("exp", h, b))

        def MASK(h, b):
            chunks = batches[b]
            eb = ebuf[b % NEB]
            WM("gpsimd", ("exp", h, b))
            n = len(chunks)
            stride = chunks[0][2]
            r0 = chunks[0][0] % 8
            assert all(s == stride for _, _, s in chunks)
            assert all((c % 8) == r0 + i for i, (c, _, _) in enumerate(chunks))
            e3 = eb[:, 0:stride * n].rearrange("p (ci m) -> p ci m", m=stride)[:, :, 0:128]
            m3 = mask8[:, r0 * 128:(r0 + n) * 128].rearrange("p (ci m) -> p ci m", m=128)
            EI("gpsimd", lambda g, e3=e3, m3=m3: g.tensor_tensor(e3, e3, m3, ALU.mult),
               "pool", 1, mark=("msk", h, b))

        def VD(h, b):
            chunks = batches[b]
            eb = ebuf[b % NEB]
            WM("tensor", ("msk", h, b))
            if b == 0:
                WM("tensor", ("oevac", h - 1))
            first = b == 0
            for ci, (c, off, s) in enumerate(chunks):
                j0 = c // 8
                base = j0 * 128
                segs = _bank_segs(base, 1024)
                for si, (s0, s1) in enumerate(segs):
                    st_ = first and ci == 0
                    e0 = off + s0 - base
                    e1 = off + s1 - base
                    EI("tensor", lambda t, c=c, s0=s0, s1=s1, e0=e0, e1=e1, st_=st_, eb=eb:
                       t.matmul(PS[2][:, s0:s1],
                                V_tm[:, c * 128:(c + 1) * 128],
                                eb[:, e0:e1],
                                start=st_, stop=False))
            nc_ = len(chunks)
            s = chunks[0][2]
            base = (chunks[0][0] // 8) * 128
            st_ = b == 0
            if s > 512:
                assert nc_ == 1
                segs = _bank_segs(base, base + s) if base + s <= 1024 else None
                segs = _bank_segs(base, 1024)
                for si, (s0, s1) in enumerate(segs):
                    EI("tensor", lambda t, s0=s0, s1=s1, base=base, st_=st_, eb=eb:
                       t.matmul(PS[3][0:1, s0:s1], onescb[:],
                                eb[:, s0 - base:s1 - base], start=st_, stop=False),
                       **({"sem": "pe", "mark": ("vd", h, b)} if si == len(segs) - 1 else {}))
            else:
                step = max(1, 512 // s)
                for g0 in range(0, nc_, step):
                    g1 = min(g0 + step, nc_)
                    last = g1 == nc_
                    if g1 - g0 == 1:
                        oap = PS[3][0:1, base:base + s]
                    else:
                        oap = PS[3][0:1, base:base + s][:, None, :].broadcast_to(
                            [1, g1 - g0, s])
                    EI("tensor", lambda t, oap=oap, g0=g0, g1=g1, s=s, st_=st_, eb=eb:
                       t.matmul(oap, onescb[:], eb[:, g0 * s:g1 * s], start=st_, stop=False),
                       **({"sem": "pe", "mark": ("vd", h, b)} if last else {}))

        def RECIP(h):
            WM("scalar", ("vd", h, NB - 1))
            WM("scalar", ("bcmm", h - 1))
            EI("scalar", lambda s: s.activation(lnrow, PS[3][0:1, 0:1024], AF.Ln), "act", 1)
            W("scalar", "act", cnt["act"])
            EI("scalar", lambda s: s.activation(rec, lnrow, AF.Exp, scale=-1.0),
               "act", 1, mark=("recip", h))

        def ENDGAME(h):
            WM("tensor", ("recip", h))
            EI("tensor", lambda t: t.matmul(PS[3][:, 0:512], onesrf[:], rec[0:1, 0:512],
                                            start=True, stop=True))
            EI("tensor", lambda t: t.matmul(PS[3][:, 512:1024], onesrf[:], rec[0:1, 512:1024],
                                            start=True, stop=True),
               "pe", 1, mark=("bcmm", h))
            WM("vector", ("bcmm", h))
            EI("vector", lambda v: v.tensor_copy(bsb, PS[3][:]))
            EI("vector", lambda v, h=h: v.tensor_tensor(qpool[:, h * 1024:(h + 1) * 1024],
                                                        PS[2][:], bsb, ALU.mult),
               "vec", 1, mark=("oevac", h))

        W("tensor", "kvld", kvld_total)
        for h in range(H):
            S(h, 0)
            S(h, 1)
            if h > 0:
                ENDGAME(h - 1)
            EXP(h, 0)
            MASK(h, 0)
            for b in range(2, NB):
                S(h, b)
                if b >= NEB:
                    VD(h, b - NEB)
                EXP(h, b - 1)
                MASK(h, b - 1)
            EXP(h, NB - 1)
            MASK(h, NB - 1)
            for b in range(NB - NEB, NB):
                VD(h, b)
            RECIP(h)
        ENDGAME(H - 1)
        DUMP("dbg_o", qpool[:])

        vd_last = marks[("vd", H - 1, NB - 1)]
        oev_last = marks[("oevac", H - 1)]
        exp_last = marks[("exp", H - 1, NB - 1)]
        W("gpsimd", *vd_last)
        W("gpsimd", *oev_last)
        EI("gpsimd", lambda g: g.dma_start(eb65, eb65_h[:]), "dma_c", 16, mark="eb65ld")
        sv, vv = marks["eb65ld"]
        W("tensor", sv, vv)
        W("tensor", *oev_last)
        W("tensor", *exp_last)
        W("sync", *vd_last)
        W("vector", *vd_last)
        nstores = 0
        for p in range(2):
            for j in range(NJ):
                ei = p * NJ + j
                par = ei % 2
                WM("sync", ("t1", ei - 2))
                EI("sync", lambda e, p=p, j=j, par=par: e.dma_start(
                    xbufE[par], xtm_own_h[j, :, p * 2048:p * 2048 + 2048]),
                   f"ds{par}", 16, mark=("xE", ei))
                sv, vv = marks[("xE", ei)]
                W("vector", sv, vv)
                if p == 1 and j == 0:
                    WM("vector", ("store", 7))
                EI("vector", lambda v, j=j, par=par: v.tensor_scalar(
                    stg[j], xbufE[par], s1a[:, j:j + 1], s2a[:, j:j + 1], ALU.mult, ALU.add),
                   "vec", 1, mark=("t1", ei))
                if debug and p == 1 and j == 0:
                    sv_, vv_ = marks[("t1", ei)]
                    W("gpsimd", sv_, vv_)
                    DUMP("dbg_e1", stg[0])
                    DUMP("dbg_s1b", s1a[:])
                    W("vector", "st", cnt["st"])
            for dg in range(8):
                dmg = p * 8 + dg
                par = dmg % 2
                pspair = (PS[0], PS[1]) if par == 0 else (PS[2], PS[3])
                WM("gpsimd", ("pp", dmg - 2))
                EI("gpsimd", lambda g, dmg=dmg, par=par: g.dma_start(
                    wo_slab[par], bigwo_h[dmg, :, :]),
                   f"dg{par}", 16, mark=("wos", dmg))
                sv, vv = marks[("wos", dmg)]
                W("tensor", sv, vv)
                WM("tensor", ("ppev", dmg - 2))
                for j in range(NJ):
                    psj = pspair[0] if j < 4 else pspair[1]
                    jo = (j % 4) * 256
                    for hh in range(H):
                        EI("tensor", lambda t, j=j, hh=hh, psj=psj, jo=jo, par=par: t.matmul(
                            psj[:, jo:jo + 256],
                            qpool[:, hh * 1024 + j * 128: hh * 1024 + (j + 1) * 128],
                            wo_slab[par][:, hh * 256:(hh + 1) * 256],
                            start=(hh == 0), stop=False))
                    EI("tensor", lambda t, j=j, psj=psj, jo=jo, dmg=dmg: t.matmul(
                        psj[:, jo:jo + 256],
                        wdT65[:, j * 128:(j + 1) * 128],
                        eb65[:, dmg * 256:(dmg + 1) * 256],
                        start=False, stop=True),
                       **({"sem": "pe", "mark": ("pp", dmg)} if j == NJ - 1 else {}))
                WM("vector", ("pp", dmg))
                for j in range(NJ):
                    psj = pspair[0] if j < 4 else pspair[1]
                    jo = (j % 4) * 256
                    kw = {"sem": "vec", "mark": ("ppev", dmg)} if j == NJ - 1 else                          ({"sem": "vec", "mark": ("ppevj", dmg, j)} if dg == 7 else {})
                    EI("vector", lambda v, j=j, dg=dg, psj=psj, jo=jo: v.tensor_tensor(
                        stg[j][:, dg * 256:(dg + 1) * 256],
                        stg[j][:, dg * 256:(dg + 1) * 256],
                        psj[:, jo:jo + 256], ALU.add), **kw)
            if debug and p == 1:
                sv, vv = marks[("ppev", p * 8 + 7)]
                W("gpsimd", sv, vv)
                DUMP("dbg_e2", stg[0])
                W("sync", "st", cnt["st"])
            for j in range(NJ):
                key = ("ppevj", p * 8 + 7, j) if j < NJ - 1 else ("ppev", p * 8 + 7)
                sv, vv = marks[key]
                W("sync", sv, vv)
                EI("sync", lambda e, p=p, j=j: e.dma_start(
                    out_h[j, :, p * 2048:p * 2048 + 2048], stg[j]),
                   "st", 16, mark=("store", nstores))
                nstores += 1
        W("sync", "st", cnt["st"])

        ninstr = sum(len(v) for v in q.values())
        print(f"[kernel build] instructions: {ninstr} "
              f"({ {k: len(v) for k, v in q.items()} })")

        block = stack.enter_context(nc.Block())

        @block.sync
        def _(e):
            for f in q["sync"]:
                f(e)

        @block.gpsimd
        def _(e):
            for f in q["gpsimd"]:
                f(e)

        @block.tensor
        def _(e):
            for f in q["tensor"]:
                f(e)

        @block.scalar
        def _(e):
            for f in q["scalar"]:
                f(e)

        @block.vector
        def _(e):
            for f in q["vector"]:
                f(e)

    return nc


def _prep_inputs(L0_attn_x, L0_attn_wq_w, L0_attn_wq_b, L0_attn_wk_w, L0_attn_wk_b,
                 L0_attn_wv_w, L0_attn_wv_b, L0_attn_wo_w, L0_attn_wo_b,
                 L0_ffn_router_weight, L0_ffn_router_bias, L0_ffn_expert_bias):
    f32 = np.float32
    x = np.ascontiguousarray(np.asarray(L0_attn_x, dtype=f32).reshape(T, D))
    wq = np.asarray(L0_attn_wq_w, dtype=f32)
    wk = np.asarray(L0_attn_wk_w, dtype=f32)
    wv = np.asarray(L0_attn_wv_w, dtype=f32)
    wo = np.asarray(L0_attn_wo_w, dtype=f32)
    rw = np.asarray(L0_ffn_router_weight, dtype=f32)
    eb = np.asarray(L0_ffn_expert_bias, dtype=f32)
    wqb_ = np.asarray(L0_attn_wq_b, dtype=f32)
    wkb_ = np.asarray(L0_attn_wk_b, dtype=f32)
    wvb_ = np.asarray(L0_attn_wv_b, dtype=f32)
    wob_ = np.asarray(L0_attn_wo_b, dtype=f32)
    rbb_ = np.asarray(L0_ffn_router_bias, dtype=f32)

    xT = np.ascontiguousarray(x.T)
    xfm_all = np.ascontiguousarray(
        xT.reshape(32, 128, NT, 128).transpose(2, 1, 0, 3).reshape(NT, 128, D))
    xtm_all = np.ascontiguousarray(x.reshape(NT, 128, D))
    wqT = np.ascontiguousarray(wq.T)
    bigwq = np.ascontiguousarray(
        wqT.reshape(32, 128, 32, 128).transpose(2, 1, 0, 3).reshape(32, 128, D))
    woT = np.ascontiguousarray(wo.T)
    bigwo = np.ascontiguousarray(
        woT.reshape(32, 128, 16, 256).transpose(2, 1, 0, 3).reshape(16, 128, 8192))
    wkt = np.ascontiguousarray(wk.T.reshape(32, 128, 128).transpose(1, 0, 2).reshape(128, D))
    wvt = np.ascontiguousarray(wv.T.reshape(32, 128, 128).transpose(1, 0, 2).reshape(128, D))
    rwt = np.ascontiguousarray(rw.T.reshape(32, 128, NE).transpose(1, 0, 2).reshape(128, 2048))
    eb65 = np.ascontiguousarray(np.concatenate([eb, wob_[None, :]], axis=0))
    wqb_in = np.ascontiguousarray(wqb_.reshape(32, 128).T)
    wkb_in = np.ascontiguousarray(wkb_.reshape(128, 1))
    wvb_in = np.ascontiguousarray(wvb_.reshape(1, 128))
    rbb_in = np.ascontiguousarray(np.broadcast_to(rbb_[None, :], (128, NE)))
    identf = np.eye(128, dtype=f32)

    tril = (np.arange(128)[:, None] <= np.arange(128)[None, :]).astype(f32)
    onesb = np.ones((128, 128), dtype=f32)
    zerosb = np.zeros((128, 128), dtype=f32)

    shared = dict(bigwq=bigwq, bigwo=bigwo,
                  wkt=wkt, wvt=wvt, rwt=rwt, eb65=eb65, wqb=wqb_in, wkb=wkb_in,
                  wvb=wvb_in, rbb=rbb_in, identf=identf)
    in_maps = []
    for i in range(NCORE):
        own = [8 * j + i for j in range(NJ)]
        m8 = np.concatenate(
            [onesb if r < i else (tril if r == i else zerosb) for r in range(8)],
            axis=1).astype(ml_dtypes.bfloat16)
        d = dict(shared)
        d["xfm_own"] = np.ascontiguousarray(xfm_all[own])
        d["xtm_own"] = np.ascontiguousarray(xtm_all[own])
        d["mask8"] = m8
        in_maps.append(d)
    return in_maps


def kernel(**inputs):
    in_maps = _prep_inputs(**inputs)
    nc = build_nc()
    res = run_bass_kernel_spmd(nc, in_maps, core_ids=list(range(NCORE)))
    full = np.empty((T, D), dtype=np.float32)
    for i, r in enumerate(res.results):
        o = r["out"]
        for j in range(NJ):
            t = 8 * j + i
            full[t * 128:(t + 1) * 128] = o[j]
    return full.reshape(1, T, D)


# version 21 (final)
# speedup vs baseline: 1.0036x; 1.0015x over previous
import math
from collections import defaultdict
from contextlib import ExitStack

import numpy as np
import ml_dtypes

import concourse.bass as bass
import concourse.mybir as mybir
from concourse.bass_utils import run_bass_kernel_spmd

F32 = mybir.dt.float32
BF16 = mybir.dt.bfloat16
AF = mybir.ActivationFunctionType
ALU = mybir.AluOpType
AX = mybir.AxisListType

D = 4096
T = 8192
H = 32
HD = 128
NT = 64
NJ = 8
NCORE = 8
NE = 64
KTOP = 8
SCL = 1.0 / math.sqrt(128.0)
EPS = 1e-5
NEGC = -12.0


def _batches():
    sizes = [(8 - c // 8) * 128 for c in range(64)]
    batches, cur, cols = [], [], 0
    for c, s in enumerate(sizes):
        if cols + s > 1024 or (cur and s != cur[0][2]):
            batches.append(cur)
            cur, cols = [], 0
        cur.append((c, cols, s))
        cols += s
    batches.append(cur)
    return batches


def build_nc(debug=False):
    nc = bass.Bass("TRN2", target_bir_lowering=False, debug=False)
    dr = lambda n, s, dt=F32, kind="ExternalInput": nc.dram_tensor(n, s, dt, kind=kind)

    xfm_own_h = dr("xfm_own", [NJ, 128, D])
    xtm_own_h = dr("xtm_own", [NJ, 128, D])
    bigwq_h = dr("bigwq", [32, 128, D])
    bigwo_h = dr("bigwo", [16, 128, 8192])
    wkt_h = dr("wkt", [128, D])
    wvt_h = dr("wvt", [128, D])
    rwt_h = dr("rwt", [128, 2048])
    eb65_h = dr("eb65", [65, D])
    wqb_h = dr("wqb", [128, 32])
    wkb_h = dr("wkb", [128, 1])
    wvb_h = dr("wvb", [1, 128])
    rbb_h = dr("rbb", [128, NE])
    identf_h = dr("identf", [128, 128])
    mask8_h = dr("mask8", [128, 1024], BF16)
    out_h = dr("out", [NJ, 128, D], F32, kind="ExternalOutput")
    kown_d = nc.dram_tensor("kown_b", [128, 1024], BF16)
    kgath_d = nc.dram_tensor("kgath_b", [1024, 1024], BF16, addr_space="Shared")
    vown_d = nc.dram_tensor("vown_b", [1024, 128], BF16)
    vgath_d = nc.dram_tensor("vgath_b", [T, 128], BF16, addr_space="Shared")
    dbg_h = {}
    if debug:
        for n, s, dt in [("dbg_rstd", [128, NJ], F32), ("dbg_nmr", [128, NJ], F32),
                         ("dbg_xn", [128, 32768], BF16), ("dbg_K", [128, T], BF16),
                         ("dbg_V", [128, T], BF16), ("dbg_q", [128, 32768], BF16),
                         ("dbg_o", [128, 32768], BF16), ("dbg_wdt", [65, 1024], BF16),
                         ("dbg_wsum", [128, NJ], F32), ("dbg_s1", [128, NJ], F32),
                         ("dbg_s2", [128, NJ], F32), ("dbg_sx", [128, 2], F32),
                         ("dbg_sxx", [128, 2], F32), ("dbg_mt", [128, 2], F32),
                         ("dbg_vt", [128, 2], F32), ("dbg_lt", [128, 2], F32),
                         ("dbg_bsb", [128, 512], F32), ("dbg_rows", [1, 512], F32),
                         ("dbg_lg", [128, 512], F32), ("dbg_snap", [128, 512], F32),
                         ("dbg_nm8", [128, NJ], F32), ("dbg_ss8", [128, NJ], F32),
                         ("dbg_rs8", [128, NJ], F32), ("dbg_e1", [128, 2048], F32),
                         ("dbg_e2", [128, 2048], F32), ("dbg_e3", [128, 256], F32),
                         ("dbg_s1b", [128, NJ], F32)]:
            dbg_h[n] = dr(n, s, dt, kind="ExternalOutput")

    with ExitStack() as stack:
        sb = lambda n, s, dt: stack.enter_context(nc.sbuf_tensor(n, s, dt))
        psa = lambda n: stack.enter_context(nc.psum_tensor(n, [128, 1024], F32))
        mksem = lambda n: stack.enter_context(nc.semaphore(n))

        identf = sb("identf_s", [128, 128], F32)
        mask8 = sb("mask8_s", [128, 1024], BF16)
        wqb = sb("wqb_s", [128, 32], F32)
        wkb = sb("wkb_s", [128, 1], F32)
        wvb = sb("wvb_s", [1, 128], BF16)
        rbb = sb("rbb_s", [128, NE], F32)
        wdT65 = sb("wdT65_s", [65, 1024], BF16)
        neg12 = sb("neg12_s", [128, 1], F32)
        onescb = sb("onescb_s", [128, 1], BF16)
        onesrb = sb("onesrb_s", [1, 128], BF16)
        onesrf = sb("onesrf_s", [1, 128], F32)
        rstd_all = sb("rstd_all_s", [128, NJ], F32)
        nmr_all = sb("nmr_all_s", [128, NJ], F32)
        wsum = sb("wsum_s", [128, NJ], F32)
        s1a = sb("s1a_s", [128, NJ], F32)
        s2a = sb("s2a_s", [128, NJ], F32)
        sx2 = sb("sx2_s", [128, 2], F32)
        sxx2 = sb("sxx2_s", [128, 2], F32)
        mt2 = sb("mt2_s", [128, 2], F32)
        mq2 = sb("mq2_s", [128, 2], F32)
        vt2 = sb("vt2_s", [128, 2], F32)
        lt2 = sb("lt2_s", [128, 2], F32)
        rst2 = sb("rst2_s", [128, 2], F32)
        nmr2 = sb("nmr2_s", [128, 2], F32)
        logits_sb = sb("logits_s", [128, NE], F32)
        if debug:
            dbg_lg_sb = sb("dbg_lg_sb", [128, 512], F32)
            dbg_snap_sb = sb("dbg_snap_sb", [128, 512], F32)
            dbg_nm8_sb = sb("dbg_nm8_sb", [128, NJ], F32)
            dbg_ss8_sb = sb("dbg_ss8_sb", [128, NJ], F32)
            dbg_rs8_sb = sb("dbg_rs8_sb", [128, NJ], F32)
        s_sb = sb("ssm_s", [128, NE], F32)
        swork = sb("swork_s", [128, NE], F32)
        keep = sb("keep_s", [128, NE], F32)
        mske = sb("mske_s", [128, NE], F32)
        wdense = sb("wdense_s", [128, NE], F32)
        m8 = sb("m8_s", [128, 1], F32)
        ssum8 = sb("ssum8_s", [128, NJ], F32)
        rsr8 = sb("rsr8_s", [128, NJ], F32)
        m8w = sb("m8w_s", [128, NJ], F32)
        nmax = sb("nmax_s", [128, 1], F32)
        ssum = sb("ssum_s", [128, 1], F32)
        rsr = sb("rsr_s", [128, 1], F32)

        K_fm = sb("K_fm", [128, T], BF16)
        V_tm = sb("V_tm", [128, T], BF16)
        qpool = sb("qpool", [128, 32768], BF16)
        xn_own = sb("xn_own", [128, 32768], BF16)
        stream = sb("stream", [128, 16384], BF16)

        qpool_f = qpool.bitcast(F32)
        xn_f = xn_own.bitcast(F32)
        stream_f = stream.bitcast(F32)
        kv_f = K_fm.bitcast(F32)

        assert list(qpool_f.shape) == [128, 16384], qpool_f.shape

        xfm_buf = [qpool[:, k * 4096:(k + 1) * 4096] for k in range(3)]
        xsq_buf = [qpool[:, 12288 + k * 4096:12288 + (k + 1) * 4096] for k in range(3)]
        rwt_bf = V_tm[:, 0:2048]
        wq_slab = [stream[:, 0:4096], stream[:, 4096:8192]]
        ebuf = [stream[:, 8192:9216], stream[:, 9216:10240], stream[:, 0:1024],
                stream[:, 1024:2048], stream[:, 2048:3072], stream[:, 3072:4096],
                stream[:, 4096:5120]]
        kown_sb = stream[:, 8192:9216]
        vown_sb = stream[:, 9216:10240]
        bsb = stream_f[:, 5120:6144]
        rec = stream_f[0:1, 6144:7168]
        lnrow = stream_f[0:1, 7168:8192]
        wo_slab = [stream[:, 0:8192], stream[:, 8192:16384]]
        stg = [xn_f[:, j * 2048:(j + 1) * 2048] for j in range(NJ)]
        xbufE = [kv_f[:, 0:2048], kv_f[:, 2048:4096]]
        eb65 = V_tm[0:65, 0:4096]
        b_sb = stream_f[:, 5120:5632]
        rows_sb = stream_f[0:1, 5632:6144]
        logits_all = stream_f[:, 5632:6144]
        s_all = stream_f[:, 4096:4608]
        swork_w = stream_f[:, 4608:5120]
        keep_w = stream_f[:, 5120:5632]

        PS = [psa(f"PS{i}") for i in range(4)]

        sems = {n: mksem(n) for n in ["ds0", "ds1", "dg0", "dg1", "dma_c", "st",
                                      "pe", "act", "vec", "pool", "coll", "kvld"]}
        cnt = defaultdict(int)
        marks = {}
        q = {"sync": [], "gpsimd": [], "tensor": [], "scalar": [], "vector": []}

        def EI(eng, bf, sem=None, k=1, mark=None):
            if eng == "vector" and sem is None:
                sem = "vec"
            if sem is not None:
                cnt[sem] += k
                if mark is not None:
                    marks[mark] = (sem, cnt[sem])
                sh = sems[sem]

                def g(e, bf=bf, sh=sh, k=k):
                    bf(e).then_inc(sh, k)
            else:
                def g(e, bf=bf):
                    bf(e)
            q[eng].append(g)
            if eng == "vector":
                W("vector", "vec", cnt["vec"])

        def W(eng, sem, v):
            if v > 0:
                q[eng].append(lambda e, sh=sems[sem], v=v: e.wait_ge(sh, v))

        def WM(eng, key):
            if key in marks:
                s, v = marks[key]
                W(eng, s, v)

        def EIG(bf, mark=None):
            EI("vector", bf, "vec", 1, mark=mark)

        def barrier():
            snap = [(s, cnt[s]) for s in sems if s not in ("coll", "kvld")]
            for eng in q:
                for s, v in snap:
                    W(eng, s, v)

        def DUMP(name, ap):
            if debug:
                EI("gpsimd", lambda g, ap=ap, h=dbg_h[name]: g.dma_start(h[:], ap), "st", 16)

        EI("sync", lambda e: e.dma_start(identf[:], identf_h[:]), "ds0", 16)
        EI("sync", lambda e: e.dma_start(mask8[:], mask8_h[:]), "ds0", 16)
        EI("sync", lambda e: e.dma_start(wqb[:], wqb_h[:]), "ds0", 16)
        EI("sync", lambda e: e.dma_start(wkb[:], wkb_h[:]), "ds0", 16)
        EI("gpsimd", lambda g: g.dma_start(wvb[:], wvb_h[:]), "dma_c", 16)
        EI("sync", lambda e: e.dma_start(rbb[:], rbb_h[:]), "ds0", 16)
        EI("gpsimd", lambda g: g.dma_start(rwt_bf, rwt_h[:]), "dma_c", 16)
        kt_bf = K_fm[:, 0:4096]
        vt_bf = K_fm[:, 4096:8192]
        EI("gpsimd", lambda g: g.dma_start(kt_bf, wkt_h[:]), "dma_c", 16)
        EI("gpsimd", lambda g: g.dma_start(vt_bf, wvt_h[:]), "dma_c", 16)
        EI("gpsimd", lambda g: g.memset(neg12[:], NEGC))
        EI("gpsimd", lambda g: g.memset(onescb[:], 1.0))
        EI("gpsimd", lambda g: g.memset(onesrb[:], 1.0))
        EI("gpsimd", lambda g: g.memset(onesrf[:], 1.0))
        EI("gpsimd", lambda g: g.memset(wdT65[64:65, :], 1.0), "dma_c", 1, mark="consts")
        cdone = cnt["dma_c"]
        sdone = cnt["ds0"]
        for eng in q:
            W(eng, "dma_c", cdone)
            W(eng, "ds0", sdone)

        batches = _batches()
        NB = len(batches)

        def WARM(n):
            for _ in range(n):
                EI("tensor", lambda t: t.matmul(PS[0][:, 0:512], mask8[:, 0:128],
                                                mask8[:, 0:512], start=True, stop=True))

        rows_b = stream_f[0:1, 6144:6656]

        def phaseA_tile(ti, own, a_or_j):
            par = ti % 2
            bpar = ti % 3
            pofs = 512 * par
            bofs = 512 * par
            idx = a_or_j
            m_row = rows_sb[0:1, 256 * par:256 * par + 128]
            v_row = rows_sb[0:1, 256 * par + 128:256 * par + 256]
            rstd_row = rows_b[0:1, 256 * par:256 * par + 128]
            nmr_row = rows_b[0:1, 256 * par + 128:256 * par + 256]

            WM("gpsimd", ("xnmul", ti - 3))
            WM("gpsimd", ("sq", ti - 3))
            EI("gpsimd", lambda g: g.dma_start(xfm_buf[bpar], xfm_own_h[idx, :, :]),
               ["dg0", "dg1", "dma_c"][bpar], 16, mark=("xfm", ti))
            gv, gvv = marks[("xfm", ti)]
            W("scalar", gv, gvv)
            WM("scalar", ("sxxm", ti - 3))
            EI("scalar", lambda s: s.activation(xsq_buf[bpar], xfm_buf[bpar], AF.Square),
               "act", 1, mark=("sq", ti))
            W("tensor", gv, gvv)
            WM("tensor", ("tbcp", ti - 2))
            oap = PS[3][0:1, pofs:pofs + 128][:, None, :].broadcast_to([1, 4, 128])
            for b in range(8):
                EI("tensor", lambda t, b=b, oap=oap: t.matmul(
                    oap, onescb[:], xfm_buf[bpar][:, b * 512:(b + 1) * 512],
                    start=(b == 0), stop=(b == 7)),
                   **({"sem": "pe", "mark": ("sxm", ti)} if b == 7 else {}))
            WM("tensor", ("sq", ti))
            oap2 = PS[3][0:1, pofs + 128:pofs + 256][:, None, :].broadcast_to([1, 4, 128])
            for b in range(8):
                EI("tensor", lambda t, b=b, oap2=oap2: t.matmul(
                    oap2, onescb[:], xsq_buf[bpar][:, b * 512:(b + 1) * 512],
                    start=(b == 0), stop=(b == 7)),
                   **({"sem": "pe", "mark": ("sxxm", ti)} if b == 7 else {}))
            WM("vector", ("sxxm", ti))
            WM("vector", ("bcmmA", ti - 2))
            EI("vector", lambda v: v.tensor_copy(rows_sb[0:1, 256 * par:256 * par + 256],
                                                 PS[3][0:1, pofs:pofs + 256]),
               "vec", 1, mark=("rowsA", ti))
            EIG(lambda v: v.tensor_scalar(m_row, m_row, 1.0 / D, None, ALU.mult))
            EIG(lambda v: v.tensor_tensor(rstd_row, m_row, m_row, ALU.mult))
            EIG(lambda v: v.tensor_scalar(v_row, v_row, 1.0 / D, None, ALU.mult))
            EIG(lambda v: v.tensor_tensor(v_row, v_row, rstd_row, ALU.subtract))
            EIG(lambda v: v.tensor_scalar(v_row, v_row, EPS, None, ALU.add),
                mark=("statA", ti))
            WM("scalar", ("statA", ti))
            EI("scalar", lambda s: s.activation(nmr_row, v_row, AF.Ln), "act", 1)
            W("scalar", "act", cnt["act"])
            EI("scalar", lambda s: s.activation(rstd_row, nmr_row, AF.Exp, scale=-0.5),
               "act", 1, mark=("rstdA", ti))
            WM("vector", ("rstdA", ti))
            EI("vector", lambda v: v.scalar_tensor_tensor(nmr_row, m_row, -1.0, rstd_row,
                                                          ALU.mult, ALU.mult),
               "vec", 1, mark=("nmrA", ti))
            WM("tensor", ("nmrA", ti))
            WM("tensor", ("bcopyA", ti - 2))
            EI("tensor", lambda t: t.matmul(PS[2][:, bofs:bofs + 256], onesrf[:],
                                            rows_b[0:1, 256 * par:256 * par + 256],
                                            start=True, stop=True),
               "pe", 1, mark=("bcmmA", ti))
            EI("tensor", lambda t: t.matmul(PS[3][:, pofs + 256:pofs + 257], rstd_row,
                                            onesrf[0:1, 0:1], start=True, stop=True))
            EI("tensor", lambda t: t.matmul(PS[3][:, pofs + 257:pofs + 258], nmr_row,
                                            onesrf[0:1, 0:1], start=True, stop=True),
               "pe", 1, mark=("tbA", ti))
            WM("vector", ("bcmmA", ti))
            WM("vector", ("xnadd", ti - 2))
            EI("vector", lambda v: v.tensor_copy(b_sb[:, 256 * par:256 * par + 256],
                                                 PS[2][:, bofs:bofs + 256]),
               "vec", 1, mark=("bcopyA", ti))
            WM("vector", ("tbA", ti))
            EI("vector", lambda v: v.tensor_copy(rstd_all[:, idx:idx + 1],
                                                 PS[3][:, pofs + 256:pofs + 257]))
            EI("vector", lambda v: v.tensor_copy(nmr_all[:, idx:idx + 1],
                                                 PS[3][:, pofs + 257:pofs + 258]),
               "vec", 1, mark=("tbcp", ti))
            x3 = (xfm_buf[bpar]).rearrange("p (c m) -> p c m", m=128)
            br = b_sb[:, 256 * par:256 * par + 128][:, None, :].broadcast_to([128, 32, 128])
            bn = b_sb[:, 256 * par + 128:256 * par + 256][:, None, :].broadcast_to([128, 32, 128])
            o3 = xn_own[:].rearrange("p (ic t) -> p ic t", t=1024)[:, :, idx * 128:(idx + 1) * 128]
            EI("vector", lambda v, o3=o3, x3=x3, br=br: v.tensor_tensor(o3, x3, br, ALU.mult),
               "vec", 1, mark=("xnmul", ti))
            EI("vector", lambda v, o3=o3, bn=bn: v.tensor_tensor(o3, o3, bn, ALU.add),
               "vec", 1, mark=("xnadd", ti))

        WARM(16)
        for j in range(NJ):
            phaseA_tile(j, True, j)
            WARM(10)
        WM("tensor", ("xnadd", NJ - 1))
        for half in range(2):
            for ic in range(32):
                EI("tensor", lambda t, half=half, ic=ic: t.matmul(
                    PS[0][:, half * 512:half * 512 + 512],
                    kt_bf[:, ic * 128:(ic + 1) * 128],
                    xn_own[:, ic * 1024 + half * 512: ic * 1024 + half * 512 + 512],
                    start=(ic == 0), stop=(ic == 31)))
        for j in range(NJ):
            for ic in range(32):
                EI("tensor", lambda t, j=j, ic=ic: t.matmul(
                    PS[1][:, j * 128:(j + 1) * 128],
                    xn_own[:, ic * 1024 + j * 128: ic * 1024 + (j + 1) * 128],
                    vt_bf[:, ic * 128:(ic + 1) * 128],
                    start=(ic == 0), stop=False))
            EI("tensor", lambda t, j=j: t.matmul(
                PS[1][:, j * 128:(j + 1) * 128], onesrb[:], wvb[:],
                start=False, stop=True),
               **({"sem": "pe", "mark": "kvA"} if j == NJ - 1 else {}))
        WM("scalar", "kvA")
        EI("scalar", lambda s: s.add(kown_sb, PS[0][:], wkb[:]), "act", 1)
        EI("scalar", lambda s: s.activation(vown_sb, PS[1][:], AF.Copy),
           "act", 1, mark="kvAe")
        WM("gpsimd", "kvAe")
        EI("gpsimd", lambda g: g.dma_start(kown_d[:], kown_sb), "dma_c", 16)
        vout = vown_d[:].rearrange("(j p) h -> p j h", p=128)
        vin = vown_sb.rearrange("p (j h) -> p j h", h=128)
        EI("gpsimd", lambda g: g.dma_start(vout, vin), "dma_c", 16, mark="kvout")
        W("gpsimd", "dma_c", cnt["dma_c"])
        EI("gpsimd", lambda g: g.collective_compute(
            "AllGather", mybir.AluOpType.bypass, replica_groups=[list(range(NCORE))],
            ins=[kown_d.ap().opt()], outs=[kgath_d.ap().opt()]), "coll", 1)
        EI("gpsimd", lambda g: g.collective_compute(
            "AllGather", mybir.AluOpType.bypass, replica_groups=[list(range(NCORE))],
            ins=[vown_d.ap().opt()], outs=[vgath_d.ap().opt()]), "coll", 1)
        DUMP("dbg_K", K_fm[:])
        DUMP("dbg_V", V_tm[:])
        DUMP("dbg_xn", xn_own[:])
        DUMP("dbg_rstd", rstd_all[:])
        DUMP("dbg_nmr", nmr_all[:])
        DUMP("dbg_sx", sx2[:])
        DUMP("dbg_sxx", sxx2[:])
        DUMP("dbg_mt", mt2[:])
        DUMP("dbg_vt", vt2[:])
        DUMP("dbg_lt", lt2[:])
        DUMP("dbg_bsb", b_sb)
        DUMP("dbg_rows", rows_sb)

        WM("tensor", ("tbcp", NJ - 1))
        WM("tensor", ("bcopyA", NJ - 1))
        for j in range(NJ):
            par = j % 2
            lofs = 512 * par
            WM("tensor", ("lgev", j - 2))
            for ic in range(32):
                EI("tensor", lambda t, ic=ic, j=j, lofs=lofs: t.matmul(
                    PS[3][:, lofs:lofs + NE],
                    xn_own[:, ic * 1024 + j * 128: ic * 1024 + (j + 1) * 128],
                    rwt_bf[:, ic * NE:(ic + 1) * NE],
                    start=(ic == 0), stop=(ic == 31)),
                   **({"sem": "pe", "mark": ("lg", j)} if ic == 31 else {}))
            WM("vector", ("lg", j))
            EI("vector", lambda v, j=j, lofs=lofs: v.scalar_tensor_tensor(
                logits_all[:, j * NE:(j + 1) * NE], PS[3][:, lofs:lofs + NE],
                1.0, rbb[:], ALU.mult, ALU.add),
               "vec", 1, mark=("lgev", j))
        W("sync", "coll", 2)
        WM("sync", "kvA")
        WM("sync", ("lg", NJ - 1))
        for t_ in range(NT):
            r_, j_ = t_ % 8, t_ // 8
            EI("sync", lambda e, t_=t_, r_=r_, j_=j_: e.dma_start(
                K_fm[:, t_ * 128:(t_ + 1) * 128],
                kgath_d[r_ * 128:(r_ + 1) * 128, j_ * 128:(j_ + 1) * 128]), "kvld", 16)
            EI("sync", lambda e, t_=t_, r_=r_, j_=j_: e.dma_start(
                V_tm[:, t_ * 128:(t_ + 1) * 128],
                vgath_d[r_ * 1024 + j_ * 128: r_ * 1024 + (j_ + 1) * 128, 0:128]),
               "kvld", 16)
        kvld_total = cnt["kvld"]
        WM("vector", ("lgev", NJ - 1))
        l3 = logits_all.rearrange("p (j e) -> p j e", e=NE)
        EIG(lambda v: v.reduce_max(wsum[:], l3, axis=AX.X))
        mxb = wsum[:][:, :, None].broadcast_to([128, NJ, NE])
        EIG(lambda v: v.tensor_tensor(l3, l3, mxb, ALU.subtract), mark="lgsub")
        WM("scalar", "lgsub")
        WM("scalar", "kvout")
        EI("scalar", lambda s: s.activation(s_all, logits_all, AF.Exp),
           "act", 1, mark="fexpw")

        for oc in range(32):
            par = oc % 2
            WM("gpsimd", ("qmm", oc - 2))
            EI("gpsimd", lambda g, oc=oc, par=par: g.dma_start(wq_slab[par], bigwq_h[oc, :, :]),
               f"dg{par}", 16, mark=("wqs", oc))
            sv, vv = marks[("wqs", oc)]
            W("tensor", sv, vv)
            if oc < 2:
                WM("tensor", "kvAe")
            WM("tensor", ("qevac", oc - 2))
            for ic in range(32):
                for half in range(2):
                    EI("tensor", lambda t, half=half, ic=ic, par=par: t.matmul(
                        PS[par][:, half * 512:half * 512 + 512],
                        wq_slab[par][:, ic * 128:(ic + 1) * 128],
                        xn_own[:, ic * 1024 + half * 512: ic * 1024 + half * 512 + 512],
                        start=(ic == 0), stop=(ic == 31)),
                       **({"sem": "pe", "mark": ("qmm", oc)} if (half == 1 and ic == 31) else {}))
            WM("scalar", ("qmm", oc))
            EI("scalar", lambda s, oc=oc, par=par: s.add(
                qpool[:, oc * 1024:(oc + 1) * 1024],
                PS[par][:], wqb[:, oc:oc + 1]),
               "act", 1, mark=("qevac", oc))
        s3 = s_all.rearrange("p (j e) -> p j e", e=NE)
        w3 = swork_w.rearrange("p (j e) -> p j e", e=NE)
        k3 = keep_w.rearrange("p (j e) -> p j e", e=NE)
        mskw = stream_f[:, 6144:6656]
        m3k = mskw.rearrange("p (j e) -> p j e", e=NE)
        wdense_w = logits_all
        WM("vector", "fexpw")
        EIG(lambda v: v.reduce_sum(ssum8[:], s3, axis=AX.X))
        EIG(lambda v: v.reciprocal(rsr8[:], ssum8[:]))
        rb3 = rsr8[:][:, :, None].broadcast_to([128, NJ, NE])
        EIG(lambda v: v.tensor_tensor(s3, s3, rb3, ALU.mult))
        EIG(lambda v: v.tensor_copy(swork_w, s_all))
        EIG(lambda v: v.memset(keep_w, 0.0))
        EIG(lambda v: v.memset(wsum[:], 0.0))
        for it in range(KTOP):
            EIG(lambda v: v.reduce_max(m8w[:], w3, axis=AX.X))
            EIG(lambda v: v.scalar_tensor_tensor(wsum[:], m8w[:], 0.0, wsum[:],
                                                 ALU.max, ALU.add))
            m3 = m8w[:][:, :, None].broadcast_to([128, NJ, NE])
            EIG(lambda v, m3=m3: v.tensor_tensor(m3k, w3, m3, ALU.is_ge))
            EIG(lambda v: v.tensor_tensor(k3, k3, m3k, ALU.add))
            EIG(lambda v: v.scalar_tensor_tensor(swork_w, mskw, -2.0, swork_w,
                                                 ALU.mult, ALU.add))
        EIG(lambda v: v.tensor_scalar(keep_w, keep_w, 1.0, None, ALU.is_ge))
        EIG(lambda v: v.tensor_tensor(wdense_w, s_all, keep_w, ALU.mult), mark="wdw")
        for j in range(NJ):
            tofs = 512 * (j % 2)
            WM("tensor", "wdw")
            WM("tensor", ("wdev", j - 2))
            EI("tensor", lambda t, j=j, tofs=tofs: t.transpose(
                PS[2][0:64, tofs:tofs + 128],
                wdense_w[:, j * 64:(j + 1) * 64], identf[:]),
               "pe", 1, mark=("wdT", j))
            WM("vector", ("wdT", j))
            EI("vector", lambda v, j=j, tofs=tofs: v.tensor_copy(
                wdT65[0:64, j * 128:(j + 1) * 128],
                PS[2][0:64, tofs:tofs + 128]),
               "vec", 1, mark=("wdev", j))
        EI("vector", lambda v: v.tensor_tensor(s1a[:], rstd_all[:], wsum[:], ALU.mult))
        EI("vector", lambda v: v.tensor_scalar(s1a[:], s1a[:], 1.0, None, ALU.add))
        EI("vector", lambda v: v.tensor_tensor(s2a[:], nmr_all[:], wsum[:], ALU.mult),
           "vec", 1, mark="epi_s")
        DUMP("dbg_wdt", wdT65[:])
        DUMP("dbg_wsum", wsum[:])
        DUMP("dbg_s1", s1a[:])
        DUMP("dbg_s2", s2a[:])
        WM("tensor", ("qevac", 31))
        WM("tensor", ("wdev", NJ - 1))
        WM("tensor", ("lgev", NJ - 1))
        WM("tensor", "epi_s")
        WM("scalar", "epi_s")
        WM("gpsimd", "epi_s")
        DUMP("dbg_q", qpool[:])

        def _bank_segs(a0, a1):
            segs = []
            for lo in range(0, 1024, 512):
                s0, s1 = max(a0, lo), min(a1, lo + 512)
                if s0 < s1:
                    segs.append((s0, s1))
            return segs

        def _prev_same_parity(b):
            return NB - 1 if (NB - 1) % 2 == b % 2 else NB - 2

        def S(h, b):
            chunks = batches[b]
            psb = PS[b % 2]
            key = ("exp", h, b - 2) if b >= 2 else ("exp", h - 1, _prev_same_parity(b))
            WM("tensor", key)
            for ci, (c, off, s) in enumerate(chunks):
                j0 = c // 8
                qbase = h * 1024 + j0 * 128
                segs = _bank_segs(off, off + s)
                for si, (s0, s1) in enumerate(segs):
                    last = (ci == len(chunks) - 1) and (si == len(segs) - 1)
                    EI("tensor", lambda t, c=c, off=off, s0=s0, s1=s1, qbase=qbase, psb=psb:
                       t.matmul(psb[:, s0:s1],
                                K_fm[:, c * 128:(c + 1) * 128],
                                qpool[:, qbase + s0 - off:qbase + s1 - off],
                                start=True, stop=True),
                       **({"sem": "pe", "mark": ("smm", h, b)} if last else {}))

        NEB = 7

        def _prev_same_modE(b):
            return NB - 1 - ((NB - 1 - b) % NEB)

        def EXP(h, b):
            chunks = batches[b]
            cols = chunks[-1][1] + chunks[-1][2]
            psb = PS[b % 2]
            eb = ebuf[b % NEB]
            key = ("vd", h, b - NEB) if b >= NEB else ("vd", h - 1, _prev_same_modE(b))
            WM("scalar", key)
            WM("scalar", ("smm", h, b))
            EI("scalar", lambda s, cols=cols, psb=psb, eb=eb:
               s.activation(eb[:, 0:cols], psb[:, 0:cols], AF.Exp, bias=neg12[:], scale=SCL),
               "act", 1, mark=("exp", h, b))

        def MASK(h, b):
            chunks = batches[b]
            eb = ebuf[b % NEB]
            WM("gpsimd", ("exp", h, b))
            n = len(chunks)
            stride = chunks[0][2]
            r0 = chunks[0][0] % 8
            assert all(s == stride for _, _, s in chunks)
            assert all((c % 8) == r0 + i for i, (c, _, _) in enumerate(chunks))
            e3 = eb[:, 0:stride * n].rearrange("p (ci m) -> p ci m", m=stride)[:, :, 0:128]
            m3 = mask8[:, r0 * 128:(r0 + n) * 128].rearrange("p (ci m) -> p ci m", m=128)
            EI("gpsimd", lambda g, e3=e3, m3=m3: g.tensor_tensor(e3, e3, m3, ALU.mult),
               "pool", 1, mark=("msk", h, b))

        def VD(h, b):
            chunks = batches[b]
            eb = ebuf[b % NEB]
            WM("tensor", ("msk", h, b))
            if b == 0:
                WM("tensor", ("oevac", h - 1))
            first = b == 0
            for ci, (c, off, s) in enumerate(chunks):
                j0 = c // 8
                base = j0 * 128
                segs = _bank_segs(base, 1024)
                for si, (s0, s1) in enumerate(segs):
                    st_ = first and ci == 0
                    e0 = off + s0 - base
                    e1 = off + s1 - base
                    EI("tensor", lambda t, c=c, s0=s0, s1=s1, e0=e0, e1=e1, st_=st_, eb=eb:
                       t.matmul(PS[2][:, s0:s1],
                                V_tm[:, c * 128:(c + 1) * 128],
                                eb[:, e0:e1],
                                start=st_, stop=False))
            nc_ = len(chunks)
            s = chunks[0][2]
            base = (chunks[0][0] // 8) * 128
            st_ = b == 0
            if s > 512:
                assert nc_ == 1
                segs = _bank_segs(base, base + s) if base + s <= 1024 else None
                segs = _bank_segs(base, 1024)
                for si, (s0, s1) in enumerate(segs):
                    EI("tensor", lambda t, s0=s0, s1=s1, base=base, st_=st_, eb=eb:
                       t.matmul(PS[3][0:1, s0:s1], onescb[:],
                                eb[:, s0 - base:s1 - base], start=st_, stop=False),
                       **({"sem": "pe", "mark": ("vd", h, b)} if si == len(segs) - 1 else {}))
            else:
                step = max(1, 512 // s)
                for g0 in range(0, nc_, step):
                    g1 = min(g0 + step, nc_)
                    last = g1 == nc_
                    if g1 - g0 == 1:
                        oap = PS[3][0:1, base:base + s]
                    else:
                        oap = PS[3][0:1, base:base + s][:, None, :].broadcast_to(
                            [1, g1 - g0, s])
                    EI("tensor", lambda t, oap=oap, g0=g0, g1=g1, s=s, st_=st_, eb=eb:
                       t.matmul(oap, onescb[:], eb[:, g0 * s:g1 * s], start=st_, stop=False),
                       **({"sem": "pe", "mark": ("vd", h, b)} if last else {}))

        def RECIP(h):
            WM("scalar", ("vd", h, NB - 1))
            WM("scalar", ("bcmm", h - 1))
            EI("scalar", lambda s: s.activation(lnrow, PS[3][0:1, 0:1024], AF.Ln), "act", 1)
            W("scalar", "act", cnt["act"])
            EI("scalar", lambda s: s.activation(rec, lnrow, AF.Exp, scale=-1.0),
               "act", 1, mark=("recip", h))

        def ENDGAME(h):
            WM("tensor", ("recip", h))
            EI("tensor", lambda t: t.matmul(PS[3][:, 0:512], onesrf[:], rec[0:1, 0:512],
                                            start=True, stop=True))
            EI("tensor", lambda t: t.matmul(PS[3][:, 512:1024], onesrf[:], rec[0:1, 512:1024],
                                            start=True, stop=True),
               "pe", 1, mark=("bcmm", h))
            WM("vector", ("bcmm", h))
            EI("vector", lambda v: v.tensor_copy(bsb, PS[3][:]))
            EI("vector", lambda v, h=h: v.tensor_tensor(qpool[:, h * 1024:(h + 1) * 1024],
                                                        PS[2][:], bsb, ALU.mult),
               "vec", 1, mark=("oevac", h))

        W("tensor", "kvld", kvld_total)
        for h in range(H):
            S(h, 0)
            S(h, 1)
            if h > 0:
                ENDGAME(h - 1)
            EXP(h, 0)
            MASK(h, 0)
            for b in range(2, NB):
                S(h, b)
                if b >= NEB:
                    VD(h, b - NEB)
                EXP(h, b - 1)
                MASK(h, b - 1)
            EXP(h, NB - 1)
            MASK(h, NB - 1)
            for b in range(NB - NEB, NB):
                VD(h, b)
            RECIP(h)
        ENDGAME(H - 1)
        DUMP("dbg_o", qpool[:])

        vd_last = marks[("vd", H - 1, NB - 1)]
        oev_last = marks[("oevac", H - 1)]
        exp_last = marks[("exp", H - 1, NB - 1)]
        W("gpsimd", *vd_last)
        W("gpsimd", *oev_last)
        EI("gpsimd", lambda g: g.dma_start(eb65, eb65_h[:]), "dma_c", 16, mark="eb65ld")
        sv, vv = marks["eb65ld"]
        W("tensor", sv, vv)
        W("tensor", *oev_last)
        W("tensor", *exp_last)
        W("sync", *vd_last)
        W("vector", *vd_last)
        nstores = 0
        for p in range(2):
            for j in range(NJ):
                ei = p * NJ + j
                par = ei % 2
                WM("sync", ("t1", ei - 2))
                EI("sync", lambda e, p=p, j=j, par=par: e.dma_start(
                    xbufE[par], xtm_own_h[j, :, p * 2048:p * 2048 + 2048]),
                   f"ds{par}", 16, mark=("xE", ei))
                sv, vv = marks[("xE", ei)]
                W("vector", sv, vv)
                if p == 1 and j == 0:
                    WM("vector", ("store", 7))
                EI("vector", lambda v, j=j, par=par: v.tensor_scalar(
                    stg[j], xbufE[par], s1a[:, j:j + 1], s2a[:, j:j + 1], ALU.mult, ALU.add),
                   "vec", 1, mark=("t1", ei))
                if debug and p == 1 and j == 0:
                    sv_, vv_ = marks[("t1", ei)]
                    W("gpsimd", sv_, vv_)
                    DUMP("dbg_e1", stg[0])
                    DUMP("dbg_s1b", s1a[:])
                    W("vector", "st", cnt["st"])
            for dg in range(8):
                dmg = p * 8 + dg
                par = dmg % 2
                pspair = (PS[0], PS[1]) if par == 0 else (PS[2], PS[3])
                WM("gpsimd", ("pp", dmg - 2))
                EI("gpsimd", lambda g, dmg=dmg, par=par: g.dma_start(
                    wo_slab[par], bigwo_h[dmg, :, :]),
                   f"dg{par}", 16, mark=("wos", dmg))
                sv, vv = marks[("wos", dmg)]
                W("tensor", sv, vv)
                WM("tensor", ("ppev", dmg - 2))
                for j in range(NJ):
                    psj = pspair[0] if j < 4 else pspair[1]
                    jo = (j % 4) * 256
                    for hh in range(H):
                        EI("tensor", lambda t, j=j, hh=hh, psj=psj, jo=jo, par=par: t.matmul(
                            psj[:, jo:jo + 256],
                            qpool[:, hh * 1024 + j * 128: hh * 1024 + (j + 1) * 128],
                            wo_slab[par][:, hh * 256:(hh + 1) * 256],
                            start=(hh == 0), stop=False))
                    EI("tensor", lambda t, j=j, psj=psj, jo=jo, dmg=dmg: t.matmul(
                        psj[:, jo:jo + 256],
                        wdT65[:, j * 128:(j + 1) * 128],
                        eb65[:, dmg * 256:(dmg + 1) * 256],
                        start=False, stop=True),
                       **({"sem": "pe", "mark": ("pp", dmg)} if j == NJ - 1 else {}))
                WM("vector", ("pp", dmg))
                for j in range(NJ):
                    psj = pspair[0] if j < 4 else pspair[1]
                    jo = (j % 4) * 256
                    kw = {"sem": "vec", "mark": ("ppev", dmg)} if j == NJ - 1 else                          ({"sem": "vec", "mark": ("ppevj", dmg, j)} if dg == 7 else {})
                    EI("vector", lambda v, j=j, dg=dg, psj=psj, jo=jo: v.tensor_tensor(
                        stg[j][:, dg * 256:(dg + 1) * 256],
                        stg[j][:, dg * 256:(dg + 1) * 256],
                        psj[:, jo:jo + 256], ALU.add), **kw)
            if debug and p == 1:
                sv, vv = marks[("ppev", p * 8 + 7)]
                W("gpsimd", sv, vv)
                DUMP("dbg_e2", stg[0])
                W("sync", "st", cnt["st"])
            for j in range(NJ):
                key = ("ppevj", p * 8 + 7, j) if j < NJ - 1 else ("ppev", p * 8 + 7)
                sv, vv = marks[key]
                W("sync", sv, vv)
                EI("sync", lambda e, p=p, j=j: e.dma_start(
                    out_h[j, :, p * 2048:p * 2048 + 2048], stg[j]),
                   "st", 16, mark=("store", nstores))
                nstores += 1
        W("sync", "st", cnt["st"])

        ninstr = sum(len(v) for v in q.values())
        print(f"[kernel build] instructions: {ninstr} "
              f"({ {k: len(v) for k, v in q.items()} })")

        block = stack.enter_context(nc.Block())

        @block.sync
        def _(e):
            for f in q["sync"]:
                f(e)

        @block.gpsimd
        def _(e):
            for f in q["gpsimd"]:
                f(e)

        @block.tensor
        def _(e):
            for f in q["tensor"]:
                f(e)

        @block.scalar
        def _(e):
            for f in q["scalar"]:
                f(e)

        @block.vector
        def _(e):
            for f in q["vector"]:
                f(e)

    return nc


def _prep_inputs(L0_attn_x, L0_attn_wq_w, L0_attn_wq_b, L0_attn_wk_w, L0_attn_wk_b,
                 L0_attn_wv_w, L0_attn_wv_b, L0_attn_wo_w, L0_attn_wo_b,
                 L0_ffn_router_weight, L0_ffn_router_bias, L0_ffn_expert_bias):
    f32 = np.float32
    x = np.ascontiguousarray(np.asarray(L0_attn_x, dtype=f32).reshape(T, D))
    wq = np.asarray(L0_attn_wq_w, dtype=f32)
    wk = np.asarray(L0_attn_wk_w, dtype=f32)
    wv = np.asarray(L0_attn_wv_w, dtype=f32)
    wo = np.asarray(L0_attn_wo_w, dtype=f32)
    rw = np.asarray(L0_ffn_router_weight, dtype=f32)
    eb = np.asarray(L0_ffn_expert_bias, dtype=f32)
    wqb_ = np.asarray(L0_attn_wq_b, dtype=f32)
    wkb_ = np.asarray(L0_attn_wk_b, dtype=f32)
    wvb_ = np.asarray(L0_attn_wv_b, dtype=f32)
    wob_ = np.asarray(L0_attn_wo_b, dtype=f32)
    rbb_ = np.asarray(L0_ffn_router_bias, dtype=f32)

    xT = np.ascontiguousarray(x.T)
    xfm_all = np.ascontiguousarray(
        xT.reshape(32, 128, NT, 128).transpose(2, 1, 0, 3).reshape(NT, 128, D))
    xtm_all = np.ascontiguousarray(x.reshape(NT, 128, D))
    wqT = np.ascontiguousarray(wq.T)
    bigwq = np.ascontiguousarray(
        wqT.reshape(32, 128, 32, 128).transpose(2, 1, 0, 3).reshape(32, 128, D))
    woT = np.ascontiguousarray(wo.T)
    bigwo = np.ascontiguousarray(
        woT.reshape(32, 128, 16, 256).transpose(2, 1, 0, 3).reshape(16, 128, 8192))
    wkt = np.ascontiguousarray(wk.T.reshape(32, 128, 128).transpose(1, 0, 2).reshape(128, D))
    wvt = np.ascontiguousarray(wv.T.reshape(32, 128, 128).transpose(1, 0, 2).reshape(128, D))
    rwt = np.ascontiguousarray(rw.T.reshape(32, 128, NE).transpose(1, 0, 2).reshape(128, 2048))
    eb65 = np.ascontiguousarray(np.concatenate([eb, wob_[None, :]], axis=0))
    wqb_in = np.ascontiguousarray(wqb_.reshape(32, 128).T)
    wkb_in = np.ascontiguousarray(wkb_.reshape(128, 1))
    wvb_in = np.ascontiguousarray(wvb_.reshape(1, 128))
    rbb_in = np.ascontiguousarray(np.broadcast_to(rbb_[None, :], (128, NE)))
    identf = np.eye(128, dtype=f32)

    tril = (np.arange(128)[:, None] <= np.arange(128)[None, :]).astype(f32)
    onesb = np.ones((128, 128), dtype=f32)
    zerosb = np.zeros((128, 128), dtype=f32)

    shared = dict(bigwq=bigwq, bigwo=bigwo,
                  wkt=wkt, wvt=wvt, rwt=rwt, eb65=eb65, wqb=wqb_in, wkb=wkb_in,
                  wvb=wvb_in, rbb=rbb_in, identf=identf)
    in_maps = []
    for i in range(NCORE):
        own = [8 * j + i for j in range(NJ)]
        m8 = np.concatenate(
            [onesb if r < i else (tril if r == i else zerosb) for r in range(8)],
            axis=1).astype(ml_dtypes.bfloat16)
        d = dict(shared)
        d["xfm_own"] = np.ascontiguousarray(xfm_all[own])
        d["xtm_own"] = np.ascontiguousarray(xtm_all[own])
        d["mask8"] = m8
        in_maps.append(d)
    return in_maps


def kernel(**inputs):
    in_maps = _prep_inputs(**inputs)
    nc = build_nc()
    res = run_bass_kernel_spmd(nc, in_maps, core_ids=list(range(NCORE)))
    full = np.empty((T, D), dtype=np.float32)
    for i, r in enumerate(res.results):
        o = r["out"]
        for j in range(NJ):
            t = 8 * j + i
            full[t * 128:(t + 1) * 128] = o[j]
    return full.reshape(1, T, D)
